# version 10
# baseline (speedup 1.0000x reference)
import math
import numpy as np
from contextlib import ExitStack

import concourse.bass as bass
import concourse.mybir as mybir
from concourse.bass_utils import run_bass_kernel_spmd
from concourse import library_config
from concourse.bass_isa import ReduceOp
from concourse.library_overlay import lower_extended_insts

F32 = mybir.dt.float32
BF16 = mybir.dt.bfloat16
F16 = mybir.dt.float16
I32 = mybir.dt.int32
ALU = mybir.AluOpType
AXL = mybir.AxisListType
AF = mybir.ActivationFunctionType

NCORE = 8
K_GATHER_FIRST = False
K_GDBG = False
K_NOGATHER = False
K_QTLIM = 10 ** 9


class Geo:

    def __init__(self, mini=False):
        self.B = 2
        self.D = 2048
        self.H = 16
        self.HPC = 4
        self.HD = 128
        self.NCI = self.D // 128
        self.CSH = self.D // NCORE
        self.TC = 512
        if mini:
            self.S = 1024
            self.INTER = 1024
            self.DOT = 1
        else:
            self.S = 4096
            self.INTER = 5504
            self.DOT = 6
        self.OT = self.INTER // 128
        self.DSH = self.DOT * 128
        self.TOWN = self.S // 4
        self.NT = self.S // self.TC
        self.QC = self.S // self.TC
        self.NTT = self.S // 128
        self.GUB = (self.OT + 3) // 4

    LN_EPS = 1e-5
    LAMBDA_INIT = 0.8 - 0.6 * math.exp(-0.3 * 12)
    SCALE = 0.125


def build(geo, phases=4, taps=()):
    g_ = geo
    B, S, D = g_.B, g_.S, g_.D
    HPC, HD, NCI, TC = g_.HPC, g_.HD, g_.NCI, g_.TC
    INTER, OT, DOT, DSH, CSH = g_.INTER, g_.OT, g_.DOT, g_.DSH, g_.CSH
    TOWN, NT, QC, NTT, GUB = g_.TOWN, g_.NT, g_.QC, g_.NTT, g_.GUB
    LN_EPS, LAMBDA_INIT, SCALE = g_.LN_EPS, g_.LAMBDA_INIT, g_.SCALE

    nc = bass.Bass("TRN2", target_bir_lowering=False, debug=False)

    d_x = nc.dram_tensor("x", [S, D], F32, kind="ExternalInput").ap()
    d_xo = nc.dram_tensor("xo", [TOWN, D], F32, kind="ExternalInput").ap()
    d_wqkv = nc.dram_tensor("wqkv", [HPC * 3 * HD, D], F32, kind="ExternalInput").ap()
    d_wo = nc.dram_tensor("wo", [D, D], F32, kind="ExternalInput").ap()
    d_lam4 = nc.dram_tensor("lam4", [4, HD], F32, kind="ExternalInput").ap()
    d_gnw = nc.dram_tensor("gnw", [8, HD], F32, kind="ExternalInput").ap()
    d_lut = nc.dram_tensor("lut", [1, 16], F32, kind="ExternalInput").ap()
    d_gidx = nc.dram_tensor("gidx", [INTER, CSH], I32, kind="ExternalInput").ap()
    d_uidx = nc.dram_tensor("uidx", [INTER, CSH], I32, kind="ExternalInput").ap()
    d_didx = nc.dram_tensor("didx", [D, DSH], I32, kind="ExternalInput").ap()
    d_out = nc.dram_tensor("out", [TOWN, D], F32, kind="ExternalOutput").ap()

    d_wot = nc.dram_tensor("wot_t", [NCI * NCI, 128, 128], BF16)
    d_agg_i = nc.dram_tensor("agg_i", [2 * OT, 128, 128], BF16)
    d_agg_o = nc.dram_tensor("agg_o", [NCORE * 2 * OT, 128, 128], BF16)
    d_agu_i = nc.dram_tensor("agu_i", [2 * OT, 128, 128], BF16)
    d_agu_o = nc.dram_tensor("agu_o", [NCORE * 2 * OT, 128, 128], BF16)
    d_agd_i = nc.dram_tensor("agd_i", [DOT * NCI, 128, 128], BF16)
    d_agd_o = nc.dram_tensor("agd_o", [NCORE * DOT * NCI, 128, 128], BF16)
    d_agn_i = nc.dram_tensor("agn_i", [HPC * HD, S], BF16)
    d_agn_o = nc.dram_tensor("agn_o", [4 * HPC * HD, S], BF16)

    tap_dram = {}
    def mktap(name, shape, dt=F32):
        if name in taps:
            tap_dram[name] = nc.dram_tensor("tap_" + name, shape, dt,
                                            kind="ExternalOutput").ap()
    mktap("qst", [128, HPC * S], BF16)
    mktap("kst", [128, HPC * S], BF16)
    mktap("vst", [128, NTT * 512], BF16)
    mktap("cbias", [128, 8], F32)
    mktap("gn", [HPC * HD, S], BF16)
    mktap("h", [128, OT * TOWN], BF16)
    mktap("gnall", [128, NCI * TOWN], BF16)
    mktap("wg", [NCORE * 2 * OT, 128, 128], BF16)

    es = ExitStack()
    SB = lambda name, shape, dt: es.enter_context(nc.sbuf_tensor(name, shape, dt))

    H_A = min(24, OT)
    big1 = SB("big1", [128, max(NCI * 1536, H_A * TOWN, 24576 if S == 4096 else 0)], BF16)
    big2_sz = max(18944, (OT - H_A) * TOWN if OT > H_A else 0) + 1024
    big2 = SB("big2", [128, big2_sz], BF16)
    qst = SB("qst", [128, max(HPC * S, NCI * TOWN)], BF16)
    kst = SB("kst", [128, max(HPC * S, 8 * D)], BF16)
    vst = SB("vst", [128, max(NTT * 512, NCI * TOWN, 2 * D + 2 * OT * 128 + 1024)], BF16)
    ident = SB("ident", [128, 128], BF16)
    identf = SB("identf", [128, 128], F32)
    dmn = SB("dmn", [128, 128], F32)
    lutc = SB("lutc", [128, 16], F32)
    lamneg = SB("lamneg", [128, 1], F32)
    gbv = SB("gbv", [128, 8], F32)
    cbias = SB("cbias", [128, 8], F32)
    ones1 = SB("ones1", [128, 1], BF16)
    nmaxb = SB("nmaxb", [128, 16], F32)
    csc = SB("csc", [128, 16], F32)
    rbc = SB("rbc", [128, 8 * 512], F32)
    smtmp = SB("smtmp", [128, 80], F32)
    epsb = SB("epsb", [128, 1], F32)
    iot = rbc[:, 0:TC].bitcast(I32)


    v_x = big2[:, 0:4096].bitcast(F32)
    v_wraw = [big2[:, 0:2048], big2[:, 2048:4096]]
    v_xntok = big2[:, 4096:6144]
    v_wotst = big2[:, 4096:6144]
    v_xnfm = big2[:, 6144:6144 + NCI * TC]
    v_scr1 = big2[:, 14336:16384]
    v_scr2 = big2[:, 16384:18432]
    v_sq = big2[:, 6144 + NCI * TC:6144 + NCI * TC + S]
    g_idx = big2[:, 0:4096].bitcast(I32)
    g_gg = big2[:, 4096:6144]
    g_w = big2[:, 6144:8192]
    s1b = big1[:, 0:4096].bitcast(F16)
    s2b = big1[:, 4096:8192].bitcast(F16)
    p1b = big1[:, 8192:12288]
    p2b = big1[:, 12288:16384]
    pt1 = big1[:, 16384:20480]
    pt2 = big1[:, 20480:24576]
    g_stA = big2[:, 8192:9216]
    g_stB = big2[:, 9216:10240]
    v_A = big2[:, 10240:10752]
    v_A2 = big2[:, 10752:11264]
    v_gn = big2[:, 11264:11776]
    g_g2s = big2[:, 11776:13824]
    R1bc = big2[:, 13824:14848].bitcast(F32)
    R2bc = big2[:, 14848:15872].bitcast(F32)
    sct512A = big2[:, 15872:16896].bitcast(F32)
    sct512B = big2[:, 16896:17920].bitcast(F32)
    rr = big2[0:1, 18432:18944].bitcast(F32)
    def h_sl(ot, lo, n):
        if ot < H_A:
            return big1[:, ot * TOWN + lo: ot * TOWN + lo + n]
        return big2[:, (ot - H_A) * TOWN + lo: (ot - H_A) * TOWN + lo + n]
    v_gt0 = big2[:, big2_sz - 1024:big2_sz - 512]
    v_gt1 = big2[:, big2_sz - 512:big2_sz]
    v_wg = [kst[:, 0:2048], kst[:, 2048:4096]]
    v_wu = [kst[:, 4096:6144], kst[:, 6144:8192]]
    v_xnown = vst[:, 0:NCI * TOWN]
    v_wot = [vst[:, 0:2048], vst[:, 2048:4096]]
    v_wdt = [vst[:, 4096:4096 + OT * 128], vst[:, 4096 + OT * 128:4096 + 2 * OT * 128]]
    v_gnall = qst[:, 0:NCI * TOWN]
    fmt = vst[:, 2 * D + 2 * OT * 128:2 * D + 2 * OT * 128 + 1024]
    v_outr = kst[:, 0:(TOWN // 128) * D]

    ps = es.enter_context(nc.psum_tensor("ps", [128, 4096], F32))
    psb = ps.ap().bitcast(BF16)
    def bank(i, w=512):
        return ps[:, i * 512:i * 512 + w]
    def bankb(i):
        return psb[:, i * 1024:(i + 1) * 1024]
    def drow(i):
        return ps[0:1, i * 512:(i + 1) * 512]

    class SemC:
        def __init__(self, h):
            self.h = h
            self.n = 0
        def inc(self, k=1):
            self.n += k
            return self.n
    names = ("small wraw wt wte wsd pre x sq ln tp xf mm qe nsq nmm nmx cb "
             "sc exp msk avd rr bc cmb a2 st ste rm rv gn gnd idx gsel gtp "
             "gte gsd cc wg fmm si h xo gna w4 fe ftp fte xl oadd out "
             "vh ch2 gh wraw2 wg2 w42 ccg ccu ccd ccn se mx ptp pte dtp rre").split()
    sm = {}
    for n_ in names:
        sm[n_] = SemC(es.enter_context(nc.semaphore("s_" + n_)))
    (s_small, s_wraw, s_wt, s_wte, s_wsd, s_pre, s_x, s_sq, s_ln, s_tp, s_xf,
     s_mm, s_qe, s_nsq, s_nmm, s_nmx, s_cb, s_sc, s_exp, s_msk, s_avd, s_rr,
     s_bc, s_cmb, s_a2, s_st, s_ste, s_rm, s_rv, s_gn, s_gnd, s_idx, s_gsel,
     s_gtp, s_gte, s_gsd, s_cc, s_wg, s_fmm, s_si, s_h, s_xo, s_gna, s_w4,
     s_fe, s_ftp, s_fte, s_xl, s_oadd, s_out, s_vh, s_ch2, s_gh,
     s_wraw2, s_wg2, s_w42, s_ccg, s_ccu, s_ccd, s_ccn, s_se, s_mx,
     s_ptp, s_pte, s_dtp, s_rre) = (sm[n_] for n_ in names)
    s_wrawP = [s_wraw, s_wraw2]
    s_wgP = [s_wg, s_wg2]
    s_w4P = [s_w4, s_w42]

    def hop(eng, s, ins):
        ins.then_inc(s.h, 1)
        eng.wait_ge(s.h, s.inc())

    class _DeferBlk:

        def __init__(self):
            self.qs = {"tensor": [], "vector": [], "scalar": [],
                       "gpsimd": [], "sync": []}

        def _deco(self, name):
            def d(fn):
                self.qs[name].append(fn)
            return d

        tensor = property(lambda self: self._deco("tensor"))
        vector = property(lambda self: self._deco("vector"))
        scalar = property(lambda self: self._deco("scalar"))
        gpsimd = property(lambda self: self._deco("gpsimd"))
        sync = property(lambda self: self._deco("sync"))

    def _replay():
        with nc.Block() as rblk:
            for eng in ("sync", "gpsimd", "tensor", "scalar", "vector"):
                fns = blk.qs[eng]
                deco = getattr(rblk, eng)
                for lo in range(0, len(fns), 200):
                    grp = fns[lo:lo + 200]
                    def body(e, grp=grp):
                        for fn in grp:
                            fn(e)
                    deco(body)

    blk = _DeferBlk()

    @blk.gpsimd
    def _(g):
        g.load_library(library_config.proxy)
        with nc.allow_non_contiguous_dma(reason="tiny transposed loads"):
            g.dma_start(out=smtmp[:, 0:4], in_=d_lam4.rearrange("h d -> d h")).then_inc(s_small.h, 16)
            g.dma_start(out=smtmp[:, 8:16], in_=d_gnw.rearrange("h d -> d h")).then_inc(s_small.h, 16)
        g.dma_start(out=lutc[0:1, :], in_=d_lut).then_inc(s_small.h, 16)
        g.wait_ge(s_small.h, 48)
        g.partition_broadcast(lutc[:, :], lutc[0:1, :]).then_inc(s_pre.h, 1)
        g.iota(iot, pattern=[[1, TC]], base=0, channel_multiplier=-1).then_inc(s_pre.h, 1)
    s_small.inc(48)
    s_pre.inc(2)
    PRE_G = s_pre.n

    @blk.vector
    def _(v, pg=PRE_G):
        v.wait_ge(s_pre.h, pg)
        v.tensor_scalar(out=ident[:, :], in0=iot[:, 0:128], scalar1=0, scalar2=None,
                        op0=ALU.is_equal)
        v.tensor_scalar(out=identf[:, :], in0=iot[:, 0:128], scalar1=0, scalar2=None,
                        op0=ALU.is_equal)
        v.tensor_scalar(out=dmn[:, :], in0=iot[:, 0:128], scalar1=0,
                        scalar2=-30000.0, op0=ALU.is_gt, op1=ALU.mult)
        v.tensor_tensor(out=smtmp[:, 4:5], in0=smtmp[:, 0:1], in1=smtmp[:, 1:2], op=ALU.mult)
        v.tensor_tensor(out=smtmp[:, 5:6], in0=smtmp[:, 2:3], in1=smtmp[:, 3:4], op=ALU.mult)
        v.memset(ones1[:, :], 1.0)
        v.memset(epsb[:, :], LN_EPS).then_inc(s_pre.h, 1)
    s_pre.inc()
    PRE_V1 = s_pre.n

    @blk.scalar
    def _(c, pv=PRE_V1):
        c.wait_ge(s_pre.h, pv)
        c.activation(out=smtmp[:, 6:7], in_=smtmp[:, 4:5], func=AF.Exp)
        c.activation(out=smtmp[:, 7:8], in_=smtmp[:, 5:6], func=AF.Exp).then_inc(s_pre.h, 1)
    s_pre.inc()
    PRE_C1 = s_pre.n

    @blk.vector
    def _(v, pc=PRE_C1):
        v.wait_ge(s_pre.h, pc)
        ins = v.tensor_tensor(out=lamneg[:, :], in0=smtmp[:, 7:8], in1=smtmp[:, 6:7], op=ALU.subtract)
        hop(v, s_vh, ins)
        v.tensor_scalar(out=gbv[:, :], in0=smtmp[:, 8:16], scalar1=(1.0 - LAMBDA_INIT),
                        scalar2=None, op0=ALU.mult)
        v.tensor_scalar(out=lamneg[:, :], in0=lamneg[:, :], scalar1=-LAMBDA_INIT,
                        scalar2=None, op0=ALU.add).then_inc(s_pre.h, 1)
    s_pre.inc()
    PRE_DONE = s_pre.n

    v_wqkvT = big1[:, 0:NCI * 1536]
    NWQ = HPC * 3
    for ft in range(NWQ):
        bi = ft % 2
        sw = s_wrawP[bi]
        @blk.gpsimd
        def _(g, ft=ft, bi=bi, wtv=s_wt.n, sw=sw):
            if ft >= 2:
                g.wait_ge(s_wt.h, wtv)
            g.dma_start(out=v_wraw[bi], in_=d_wqkv[ft * 128:(ft + 1) * 128, :]).then_inc(sw.h, 16)
        sw.inc(16)
        @blk.tensor
        def _(t, ft=ft, bi=bi, wv=sw.n, pv=PRE_DONE, wte=s_wte.n, sw=sw):
            t.wait_ge(s_pre.h, pv)
            t.wait_ge(sw.h, wv)
            if ft >= 1:
                t.wait_ge(s_wte.h, wte)
            ins = None
            for ci in range(NCI):
                ins = t.transpose(bankb(4 + ci // 8)[:, (ci % 8) * 128:(ci % 8 + 1) * 128],
                                  in_=v_wraw[bi][:, ci * 128:(ci + 1) * 128],
                                  identity=ident[:, :])
            ins.then_inc(s_wt.h, 1)
        s_wt.inc()
        @blk.scalar
        def _(c, ft=ft, tv=s_wt.n):
            c.wait_ge(s_wt.h, tv)
            for half in range(2):
                src = bankb(4 + half).rearrange("p (s q) -> p s q", q=128)
                dst = v_wqkvT.rearrange("p (ci f) -> p ci f", f=1536)[
                    :, 8 * half:8 * half + 8, ft * 128:(ft + 1) * 128]
                c.copy(out=dst, in_=src).then_inc(s_wte.h, 1)
        s_wte.inc(2)
    for et in range(NCI):
        bi = et % 2
        sw = s_wrawP[(NWQ + et) % 2]
        @blk.gpsimd
        def _(g, et=et, bi=bi, wtv=s_wt.n, sw=sw, nwq=NWQ):
            g.wait_ge(s_wt.h, wtv if et >= 2 else nwq)
            g.dma_start(out=v_wraw[bi], in_=d_wo[et * 128:(et + 1) * 128, :]).then_inc(sw.h, 16)
        sw.inc(16)
        @blk.tensor
        def _(t, et=et, bi=bi, wv=sw.n, wte=s_wte.n, sw=sw):
            t.wait_ge(sw.h, wv)
            t.wait_ge(s_wte.h, wte)
            ins = None
            for fc in range(NCI):
                ins = t.transpose(bankb(4 + fc // 8)[:, (fc % 8) * 128:(fc % 8 + 1) * 128],
                                  in_=v_wraw[bi][:, fc * 128:(fc + 1) * 128],
                                  identity=ident[:, :])
            ins.then_inc(s_wt.h, 1)
        s_wt.inc()
        @blk.scalar
        def _(c, et=et, tv=s_wt.n, sd=s_wsd.n):
            c.wait_ge(s_wt.h, tv)
            if et >= 1:
                c.wait_ge(s_wsd.h, sd)
            for half in range(2):
                src = bankb(4 + half).rearrange("p (s q) -> p s q", q=128)
                dst = v_wotst.rearrange("p (s q) -> p s q", q=128)[:, 8 * half:8 * half + 8, :]
                c.copy(out=dst, in_=src).then_inc(s_wte.h, 1)
        s_wte.inc(2)
        @blk.sync
        def _(sy, et=et, ev=s_wte.n):
            sy.wait_ge(s_wte.h, ev)
            src = v_wotst.rearrange("p (fc q) -> p fc q", q=128)
            dst = d_wot.rearrange("(fc et) p q -> fc et p q", et=NCI)[:, et].rearrange(
                "fc p q -> p fc q")
            sy.dma_start(out=dst, in_=src).then_inc(s_wsd.h, 16)
        s_wsd.inc(16)
    P0_WT = s_wt.n
    P0_WTE = s_wte.n
    P0_WSD = s_wsd.n

    for tcn in range(NT):
        QE_BASE = tcn * 12
        for tt4 in range(4):
            tg = tcn * 4 + tt4
            @blk.sync
            def _(sy, tg=tg, lnv=s_ln.n, sqv=s_sq.n, p0=P0_WT):
                if tg == 0:
                    sy.wait_ge(s_wt.h, p0)
                else:
                    sy.wait_ge(s_ln.h, lnv)
                    sy.wait_ge(s_sq.h, sqv)
                sy.dma_start(out=v_x[:, 0:1024],
                             in_=d_x[tg * 128:(tg + 1) * 128, 0:1024]).then_inc(s_x.h, 16)
            @blk.scalar
            def _(c, tg=tg, lnv=s_ln.n, sqv=s_sq.n, p0=P0_WT):
                if tg == 0:
                    c.wait_ge(s_wt.h, p0)
                else:
                    c.wait_ge(s_ln.h, lnv)
                    c.wait_ge(s_sq.h, sqv)
                c.dma_start(out=v_x[:, 1024:2048],
                            in_=d_x[tg * 128:(tg + 1) * 128, 1024:2048]).then_inc(s_x.h, 16)
            s_x.inc(32)
            xv = s_x.n
            @blk.scalar
            def _(c, xv=xv, sq=s_sq.n):
                c.wait_ge(s_x.h, xv)
                c.activation(out=v_scr2, in_=v_x[:, :], func=AF.Square,
                             accum_out=csc[:, 1:2]).then_inc(s_sq.h, 1)
            s_sq.inc()
            @blk.vector
            def _(v, xv=xv, sq=s_sq.n):
                v.wait_ge(s_x.h, xv)
                ins = v.tensor_scalar(out=v_scr1, in0=v_x[:, :], scalar1=1.0, scalar2=None,
                                      op0=ALU.mult, op1=ALU.add, accum_out=csc[:, 0:1])
                hop(v, s_vh, ins)
                v.wait_ge(s_sq.h, sq)
                v.tensor_scalar(out=csc[:, 2:3], in0=csc[:, 0:1], scalar1=1.0 / D,
                                scalar2=None, op0=ALU.mult)
                ins = v.tensor_scalar(out=csc[:, 3:4], in0=csc[:, 1:2], scalar1=1.0 / D,
                                      scalar2=None, op0=ALU.mult)
                hop(v, s_vh, ins)
                ins = v.tensor_tensor(out=csc[:, 4:5], in0=csc[:, 2:3], in1=csc[:, 2:3], op=ALU.mult)
                hop(v, s_vh, ins)
                v.tensor_tensor(out=csc[:, 6:7], in0=csc[:, 3:4], in1=csc[:, 4:5],
                                op=ALU.subtract).then_inc(s_rv.h, 1)
            s_rv.inc()
            @blk.scalar
            def _(c, rv=s_rv.n):
                c.wait_ge(s_rv.h, rv)
                ins = c.activation(out=csc[:, 7:8], in_=csc[:, 6:7], func=AF.Ln,
                                   bias=epsb[:, 0:1])
                hop(c, s_ch2, ins)
                c.activation(out=csc[:, 8:9], in_=csc[:, 7:8], func=AF.Exp,
                             scale=-0.5).then_inc(s_rv.h, 1)
            s_rv.inc()
            @blk.vector
            def _(v, rv=s_rv.n, tg=tg, tpv=s_tp.n, p0d=P0_WSD):
                v.wait_ge(s_rv.h, rv)
                ins = v.scalar_tensor_tensor(out=csc[:, 10:11], in0=csc[:, 8:9],
                                             scalar=-1.0, in1=csc[:, 2:3],
                                             op0=ALU.mult, op1=ALU.mult)
                hop(v, s_vh, ins)
                if tg == 0:
                    v.wait_ge(s_wsd.h, p0d)
                else:
                    v.wait_ge(s_tp.h, tpv)
                v.tensor_scalar(out=v_xntok, in0=v_x[:, :], scalar1=csc[:, 8:9],
                                scalar2=csc[:, 10:11], op0=ALU.mult, op1=ALU.add).then_inc(s_ln.h, 1)
            s_ln.inc()
            @blk.tensor
            def _(t, lnv=s_ln.n, tt4=tt4, tg=tg, mmv=s_mm.n, xfv=s_xf.n, p0e=P0_WTE):
                t.wait_ge(s_ln.h, lnv)
                if tg == 0:
                    t.wait_ge(s_wte.h, p0e)
                if tt4 == 0 and tg >= 4:
                    t.wait_ge(s_mm.h, mmv)
                if tg >= 1:
                    t.wait_ge(s_xf.h, xfv)
                for ci in range(NCI):
                    ins = t.transpose(bankb(4 + ci // 8)[:, (ci % 8) * 128:(ci % 8 + 1) * 128],
                                      in_=v_xntok[:, ci * 128:(ci + 1) * 128],
                                      identity=ident[:, :])
                ins.then_inc(s_tp.h, 1)
            s_tp.inc()
            @blk.scalar
            def _(c, tv=s_tp.n, tt4=tt4):
                c.wait_ge(s_tp.h, tv)
                for half in range(2):
                    src = bankb(4 + half).rearrange("p (s q) -> p s q", q=128)
                    dst = v_xnfm.rearrange("p (ci t) -> p ci t", t=TC)[
                        :, 8 * half:8 * half + 8, tt4 * 128:(tt4 + 1) * 128]
                    c.copy(out=dst, in_=src).then_inc(s_xf.h, 1)
            s_xf.inc(2)
        XF_CHUNK = s_xf.n

        for f8 in range(8):
            hh_ = f8 // 2
            qk = f8 % 2
            bk = f8 % 2
            @blk.tensor
            def _(t, hh_=hh_, qk=qk, bk=bk, f8=f8, xf=XF_CHUNK, qb=QE_BASE):
                t.wait_ge(s_xf.h, xf)
                t.wait_ge(s_qe.h, qb + max(0, f8 - 1))
                fo = (qk * HPC + hh_) * 128
                for ci in range(NCI):
                    ins = t.matmul(bank(bk), lhsT=v_wqkvT[:, ci * 1536 + fo:ci * 1536 + fo + 128],
                                   rhs=v_xnfm[:, ci * TC:(ci + 1) * TC],
                                   start=(ci == 0), stop=(ci == NCI - 1))
                ins.then_inc(s_mm.h, 1)
            s_mm.inc()
            @blk.vector
            def _(v, mv=s_mm.n, hh_=hh_, qk=qk, bk=bk, tcn=tcn):
                v.wait_ge(s_mm.h, mv)
                dst = (qst if qk == 0 else kst)[:, hh_ * S + tcn * TC: hh_ * S + (tcn + 1) * TC]
                v.tensor_scalar(out=dst, in0=bank(bk), scalar1=(SCALE if qk == 0 else 1.0),
                                scalar2=None, op0=ALU.mult).then_inc(s_qe.h, 1)
            s_qe.inc()
        for tt4 in range(4):
            bk = 2 + tt4 % 2
            @blk.tensor
            def _(t, tt4=tt4, bk=bk, xf=XF_CHUNK, qb=QE_BASE, tcn=tcn):
                t.wait_ge(s_xf.h, xf)
                if tt4 >= 2:
                    t.wait_ge(s_qe.h, qb + 9 + (tt4 - 2))
                elif tcn >= 1:
                    t.wait_ge(s_qe.h, qb - 1 + tt4)
                for ci in range(NCI):
                    ins = t.matmul(bank(bk), lhsT=v_xnfm[:, ci * TC + tt4 * 128:ci * TC + (tt4 + 1) * 128],
                                   rhs=v_wqkvT[:, ci * 1536 + 1024:ci * 1536 + 1536],
                                   start=(ci == 0), stop=(ci == NCI - 1))
                ins.then_inc(s_mm.h, 1)
            s_mm.inc()
            @blk.vector
            def _(v, mv=s_mm.n, tt4=tt4, bk=bk, tcn=tcn):
                v.wait_ge(s_mm.h, mv)
                tg = tcn * 4 + tt4
                v.tensor_scalar(out=vst[:, tg * 512:(tg + 1) * 512], in0=bank(bk),
                                scalar1=1.0, scalar2=None, op0=ALU.mult).then_inc(s_qe.h, 1)
            s_qe.inc()
    P1_MM = s_mm.n
    P1_QE = s_qe.n

    P1_LN = s_ln.n
    P1_SQ = s_sq.n
    P1_TP = s_tp.n
    XF_ALL = s_xf.n
    CB = 0

    GBLK = [(m, b2) for m in range(2) for b2 in range(GUB)] + \
           [(2, b2) for b2 in range(NCI)]
    n_gblk = len(GBLK)

    slot_guard = [(s_xf, XF_ALL), (s_wte, s_wte.n)]
    slot_state = [0]

    def slotb(i):
        return bankb(4 + 2 * i)

    def slotf(i):
        return ps[:, (4 + 2 * i) * 512:(4 + 2 * i) * 512 + 512]

    def alloc_slot(pair=False):
        i = slot_state[0] % 2
        slot_state[0] += 1
        return i, [slot_guard[i]]

    def emit_gather_block(gb_i):
        m, bi2 = GBLK[gb_i]
        idx_d = (d_gidx, d_uidx, d_didx)[m]
        if m < 2:
            n_ot = min(4, OT - bi2 * 4)
            n = n_ot * CSH
            nblk = n_ot * 2
        else:
            n_ot = 1
            n = DSH
            nblk = DOT
        gst = (g_stA, g_stB)[gb_i % 2]
        @blk.sync
        def _(sy, m=m, bi2=bi2, n_ot=n_ot, n=n, idx_d=idx_d, sel=s_gsel.n,
              gb_i=gb_i, lnp=P1_LN, sqp=P1_SQ):
            if gb_i == 0:
                sy.wait_ge(s_ln.h, lnp)
                sy.wait_ge(s_sq.h, sqp)
            else:
                sy.wait_ge(s_gsel.h, sel)
            if m < 2:
                src = idx_d[bi2 * 512:bi2 * 512 + n_ot * 128, :].rearrange(
                    "(o p) c -> p o c", p=128)
                dst = g_idx[:, 0:n].rearrange("p (o c) -> p o c", o=n_ot)
            else:
                src = idx_d[bi2 * 128:(bi2 + 1) * 128, :]
                dst = g_idx[:, 0:n]
            sy.dma_start(out=dst, in_=src).then_inc(s_idx.h, 16)
        s_idx.inc(16)
        @blk.vector
        def _(v, n=n, iv=s_idx.n, gtp=s_gtp.n, gb_i=gb_i, mmp=P1_MM,
              tpp=P1_TP):
            v.wait_ge(s_idx.h, iv)
            if gb_i == 0:
                v.wait_ge(s_mm.h, mmp)
                v.wait_ge(s_tp.h, tpp)
            else:
                v.wait_ge(s_gtp.h, gtp)
            ins = v.tensor_copy(out=g_gg[:, 0:n], in_=g_idx[:, 0:n])
            hop(v, s_vh, ins)
            v.tensor_scalar(out=g_w[:, 0:n], in0=g_gg[:, 0:n], scalar1=0,
                            scalar2=lutc[:, 0:1], op0=ALU.is_equal, op1=ALU.mult)
            for j in range(1, 16):
                ins = v.tensor_scalar(out=g_g2s[:, 0:n], in0=g_gg[:, 0:n],
                                      scalar1=j, scalar2=lutc[:, j:j + 1],
                                      op0=ALU.is_equal, op1=ALU.mult)
                hop(v, s_vh, ins)
                ins = v.tensor_tensor(out=g_w[:, 0:n], in0=g_w[:, 0:n],
                                      in1=g_g2s[:, 0:n], op=ALU.add)
                if j < 15:
                    hop(v, s_vh, ins)
            ins.then_inc(s_gsel.h, 1)
        s_gsel.inc()
        sl, guards = alloc_slot()
        @blk.tensor
        def _(t, m=m, n_ot=n_ot, nblk=nblk, sel=s_gsel.n, sl=sl,
              guards=tuple(guards)):
            t.wait_ge(s_gsel.h, sel)
            for sem, cnt in guards:
                t.wait_ge(sem.h, cnt)
            bb = slotb(sl)
            ins = None
            for ci in range(nblk):
                if m < 2:
                    o_, ct_ = ci % n_ot, ci // n_ot
                    src = g_w[:, o_ * CSH + ct_ * 128:o_ * CSH + ct_ * 128 + 128]
                else:
                    src = g_w[:, ci * 128:(ci + 1) * 128]
                ins = t.transpose(bb[:, ci * 128:(ci + 1) * 128], in_=src,
                                  identity=ident[:, :])
            ins.then_inc(s_gtp.h, 1)
        s_gtp.inc()
        @blk.scalar
        def _(c, nblk=nblk, tv=s_gtp.n, sl=sl, gsd=s_gsd.n, gb_i=gb_i,
              gst=gst):
            c.wait_ge(s_gtp.h, tv)
            if gb_i >= 2:
                c.wait_ge(s_gsd.h, gsd)
            bb = slotb(sl)
            c.copy(out=gst[:, 0:nblk * 128],
                   in_=bb[:, 0:nblk * 128]).then_inc(s_gte.h, 1)
        s_gte.inc()
        slot_guard[sl] = (s_gte, s_gte.n)
        @blk.sync
        def _(sy, m=m, bi2=bi2, n_ot=n_ot, nblk=nblk, ge=s_gte.n, gst=gst):
            sy.wait_ge(s_gte.h, ge)
            if m < 2:
                dd = (d_agg_i, d_agu_i)[m]
                for ct_ in range(2):
                    dst = dd.ap().rearrange("(ct o) p q -> p ct o q", o=OT)[
                        :, ct_, bi2 * 4:bi2 * 4 + n_ot]
                    src = gst[:, ct_ * n_ot * 128:(ct_ + 1) * n_ot * 128].rearrange(
                        "p (o q) -> p o q", q=128)
                    sy.dma_start(out=dst, in_=src).then_inc(s_gsd.h, 16)
            else:
                dst = d_agd_i.ap().rearrange("(o e) p q -> p o e q", e=NCI)[
                    :, :, bi2]
                src = gst[:, 0:nblk * 128].rearrange("p (o q) -> p o q", q=128)
                sy.dma_start(out=dst, in_=src).then_inc(s_gsd.h, 16)
        s_gsd.inc(32 if m < 2 else 16)
        last = (m < 2 and bi2 == GUB - 1) or (m == 2 and bi2 == NCI - 1)
        if last:
            ccs = (s_ccg, s_ccu, s_ccd)[m]
            @blk.gpsimd
            def _(g, m=m, sd=s_gsd.n, ccs=ccs):
                g.wait_ge(s_gsd.h, sd)
                pr = [(d_agg_i, d_agg_o), (d_agu_i, d_agu_o),
                      (d_agd_i, d_agd_o)][m]
                g.collective_compute(
                    "AllGather", ALU.bypass,
                    replica_groups=[list(range(NCORE))],
                    ins=[pr[0].ap().opt()], outs=[pr[1].ap().opt()],
                ).then_inc(ccs.h, 1)
            ccs.inc()

    st2 = {"blk": 0, "first": True, "gb": 0}

    def emit_scores(hh_, qt):
        nk = qt + 1
        nb = (nk + 3) // 4
        EXPP = s_exp.n
        for kb in range(nb):
            w = min(512, nk * 128 - kb * 512)
            diag = (kb == nb - 1)
            bi = st2["blk"]
            @blk.tensor
            def _(t, hh_=hh_, qt=qt, kb=kb, w=w, bi=bi, qe=P1_QE):
                if bi < 2:
                    t.wait_ge(s_qe.h, qe)
                else:
                    t.wait_ge(s_se.h, 2 * bi - 2)
                ins = None
                for half in range(2):
                    ins = t.matmul(ps[:, (2 * (bi % 2) + half) * 512:
                                      (2 * (bi % 2) + half) * 512 + w],
                                   lhsT=qst[64 * half:64 * half + 64,
                                            hh_ * S + qt * 128:hh_ * S + (qt + 1) * 128],
                                   rhs=kst[64 * half:64 * half + 64,
                                           hh_ * S + kb * 512:hh_ * S + kb * 512 + w],
                                   start=True, stop=True,
                                   tile_position=(64 * half, 0))
                ins.then_inc(s_sc.h, 1)
            s_sc.inc()
            if diag:
                @blk.vector
                def _(v, kb=kb, w=w, sc=s_sc.n, bi=bi):
                    v.wait_ge(s_sc.h, sc)
                    if bi >= 2:
                        v.wait_ge(s_se.h, 2 * bi - 2)
                    ins = None
                    for half in range(2):
                        bk_ = ps[:, (2 * (bi % 2) + half) * 512 + w - 128:
                                 (2 * (bi % 2) + half) * 512 + w]
                        ins = v.tensor_tensor(out=bk_, in0=bk_, in1=dmn[:, :],
                                              op=ALU.add)
                    ins.then_inc(s_msk.h, 1)
                s_msk.inc()
            @blk.vector
            def _(v, kb=kb, w=w, sc=s_sc.n, mk=s_msk.n, ep=EXPP, bi=bi,
                  mxp=s_mx.n, first=st2["first"], mmp=P1_MM):
                v.wait_ge(s_sc.h, sc)
                v.wait_ge(s_msk.h, mk)
                v.wait_ge(s_exp.h, ep)
                if kb == 0:
                    v.wait_ge(s_mx.h, mxp)
                if first:
                    v.wait_ge(s_mm.h, mmp)
                v.tensor_scalar(out=s1b[:, kb * 512:kb * 512 + w],
                                in0=ps[:, (2 * (bi % 2)) * 512:
                                       (2 * (bi % 2)) * 512 + w],
                                scalar1=1.0, scalar2=None, op0=ALU.mult,
                                op1=ALU.max,
                                accum_out=smtmp[:, 16 + kb:17 + kb]).then_inc(s_se.h, 1)
                v.tensor_scalar(out=s2b[:, kb * 512:kb * 512 + w],
                                in0=ps[:, (2 * (bi % 2) + 1) * 512:
                                       (2 * (bi % 2) + 1) * 512 + w],
                                scalar1=1.0, scalar2=None, op0=ALU.mult,
                                op1=ALU.max,
                                accum_out=smtmp[:, 24 + kb:25 + kb]).then_inc(s_se.h, 1)
            s_se.inc(2)
            st2["first"] = False
            st2["blk"] += 1
        @blk.vector
        def _(v, nb=nb, sev=s_se.n, expp=EXPP, lnp=P1_LN):
            v.wait_ge(s_se.h, sev)
            v.wait_ge(s_exp.h, expp)
            v.wait_ge(s_ln.h, lnp)
            v.tensor_reduce(out=csc[:, 11:12], in_=smtmp[:, 16:16 + nb],
                            axis=AXL.X, op=ALU.max, negate=True)
            v.tensor_reduce(out=csc[:, 12:13], in_=smtmp[:, 24:24 + nb],
                            axis=AXL.X, op=ALU.max, negate=True).then_inc(s_mx.h, 1)
        s_mx.inc()
        return s_mx.n

    def emit_exp(hh_, qt, mx_n):
        nk = qt + 1
        @blk.scalar
        def _(c, nk=nk, mx=mx_n, ptpp=s_ptp.n, rrp=s_rr.n, mmp=P1_MM,
              expn=s_exp.n):
            c.wait_ge(s_mx.h, mx)
            c.wait_ge(s_ptp.h, ptpp)
            c.wait_ge(s_rr.h, rrp)
            if expn == 0:
                c.wait_ge(s_mm.h, mmp)
            c.activation(out=p1b[:, 0:nk * 128], in_=s1b[:, 0:nk * 128],
                         func=AF.Exp, bias=csc[:, 11:12],
                         accum_out=csc[:, 13:14]).then_inc(s_exp.h, 1)
            c.activation(out=p2b[:, 0:nk * 128], in_=s2b[:, 0:nk * 128],
                         func=AF.Exp, bias=csc[:, 12:13],
                         accum_out=csc[:, 14:15]).then_inc(s_exp.h, 1)
        s_exp.inc(2)

    def emit_post(hh_, qt):
        nk = qt + 1
        qtc = qt % 4
        qc = qt // 4
        @blk.vector
        def _(v, ex=s_exp.n, dtp=s_dtp.n, lnp=P1_LN, rrn=s_rr.n):
            v.wait_ge(s_exp.h, ex)
            v.wait_ge(s_dtp.h, dtp)
            if rrn == 0:
                v.wait_ge(s_ln.h, lnp)
            v.reciprocal(out=csc[:, 15:16], in_=csc[:, 13:14])
            v.reciprocal(out=csc[:, 10:11], in_=csc[:, 14:15]).then_inc(s_rr.h, 1)
        s_rr.inc()
        EXP_QT = s_exp.n
        AVD_PREV = s_avd.n
        for half, pb, ptb in ((0, p1b, pt1), (1, p2b, pt2)):
            first_evac = True
            for b0 in range(0, nk, 8):
                jn = min(8, nk - b0)
                sl, guards = alloc_slot()
                @blk.tensor
                def _(t, b0=b0, jn=jn, sl=sl, guards=tuple(guards), ex=EXP_QT,
                      pb=pb):
                    t.wait_ge(s_exp.h, ex)
                    for sem, cnt in guards:
                        t.wait_ge(sem.h, cnt)
                    ins = None
                    for j in range(jn):
                        ins = t.transpose(slotb(sl)[:, j * 128:(j + 1) * 128],
                                          in_=pb[:, (b0 + j) * 128:(b0 + j + 1) * 128],
                                          identity=ident[:, :])
                    ins.then_inc(s_ptp.h, 1)
                s_ptp.inc()
                @blk.scalar
                def _(c, b0=b0, jn=jn, sl=sl, tv=s_ptp.n, avdp=AVD_PREV,
                      first=first_evac, ptb=ptb, mmp=P1_MM, pten=s_pte.n):
                    c.wait_ge(s_ptp.h, tv)
                    if first:
                        c.wait_ge(s_avd.h, avdp)
                    if pten == 0:
                        c.wait_ge(s_mm.h, mmp)
                    c.copy(out=ptb[:, b0 * 128:(b0 + jn) * 128],
                           in_=slotb(sl)[:, 0:jn * 128]).then_inc(s_pte.h, 1)
                s_pte.inc()
                slot_guard[sl] = (s_pte, s_pte.n)
                first_evac = False
        sl, guards = alloc_slot()
        @blk.tensor
        def _(t, rrv=s_rr.n, sl=sl, guards=tuple(guards)):
            t.wait_ge(s_rr.h, rrv)
            for sem, cnt in guards:
                t.wait_ge(sem.h, cnt)
            t.transpose(slotf(sl)[0:1, 0:128], in_=csc[:, 15:16],
                        identity=identf[:, :])
            t.transpose(slotf(sl)[0:1, 128:256], in_=csc[:, 10:11],
                        identity=identf[:, :]).then_inc(s_dtp.h, 1)
        s_dtp.inc()
        @blk.scalar
        def _(c, dtp=s_dtp.n, bc=s_bc.n, sl=sl):
            c.wait_ge(s_dtp.h, dtp)
            c.wait_ge(s_bc.h, bc)
            c.copy(out=rr, in_=slotf(sl)[0:1, 0:256]).then_inc(s_rre.h, 1)
        s_rre.inc()
        slot_guard[sl] = (s_rre, s_rre.n)
        @blk.gpsimd
        def _(g, rre=s_rre.n, qtc=qtc, cmbp=s_cmb.n):
            g.wait_ge(s_rre.h, rre)
            if qtc == 0:
                g.wait_ge(s_cmb.h, cmbp)
            g.partition_broadcast(R1bc[:, qtc * 128:(qtc + 1) * 128],
                                  rr[0:1, 0:128])
            g.partition_broadcast(R2bc[:, qtc * 128:(qtc + 1) * 128],
                                  rr[0:1, 128:256]).then_inc(s_bc.h, 1)
        s_bc.inc()
        @blk.tensor
        def _(t, hh_=hh_, nk=nk, qtc=qtc, pte=s_pte.n, cmbp=s_cmb.n):
            t.wait_ge(s_pte.h, pte)
            if qtc == 0:
                t.wait_ge(s_cmb.h, cmbp)
            ins = None
            for kt in range(nk):
                vt = vst[:, kt * 512 + hh_ * 128:kt * 512 + (hh_ + 1) * 128]
                st_ = (qtc == 0 and kt == 0)
                sp_ = (qtc == 3 and kt == nk - 1)
                t.matmul(ps[:, 5 * 512 + qtc * 128:5 * 512 + (qtc + 1) * 128],
                         lhsT=vt, rhs=pt1[:, kt * 128:(kt + 1) * 128],
                         start=st_, stop=sp_)
                ins = t.matmul(ps[:, 7 * 512 + qtc * 128:7 * 512 + (qtc + 1) * 128],
                               lhsT=vt, rhs=pt2[:, kt * 128:(kt + 1) * 128],
                               start=st_, stop=sp_)
            ins.then_inc(s_avd.h, 1)
        s_avd.inc()
        if qtc != 3:
            return
        @blk.vector
        def _(v, av=s_avd.n, bc=s_bc.n, a2=s_a2.n, gnp=s_gn.n, lnp=P1_LN,
              sqp=P1_SQ, xfp=XF_ALL, cmbn=s_cmb.n):
            v.wait_ge(s_avd.h, av)
            v.wait_ge(s_bc.h, bc)
            v.wait_ge(s_a2.h, a2)
            v.wait_ge(s_gn.h, gnp)
            if cmbn == 0:
                v.wait_ge(s_ln.h, lnp)
                v.wait_ge(s_sq.h, sqp)
                v.wait_ge(s_xf.h, xfp)
            ins = v.tensor_tensor(out=sct512A, in0=ps[:, 5 * 512:6 * 512],
                                  in1=R1bc, op=ALU.mult)
            hop(v, s_vh, ins)
            ins = v.tensor_tensor(out=sct512B, in0=ps[:, 7 * 512:8 * 512],
                                  in1=R2bc, op=ALU.mult)
            hop(v, s_vh, ins)
            v.scalar_tensor_tensor(out=v_A, in0=sct512B, scalar=lamneg[:, :],
                                   in1=sct512A, op0=ALU.mult,
                                   op1=ALU.add).then_inc(s_cmb.h, 1)
        s_cmb.inc()
        @blk.gpsimd
        def _(g, cm=s_cmb.n, stv=s_st.n):
            g.wait_ge(s_cmb.h, cm)
            g.wait_ge(s_st.h, stv)
            g.tensor_tensor(out=v_A2, in0=v_A, in1=v_A, op=ALU.mult).then_inc(s_a2.h, 1)
        s_a2.inc()
        sg = tuple(set(slot_guard))
        @blk.tensor
        def _(t, a2=s_a2.n, sg=sg):
            t.wait_ge(s_a2.h, a2)
            for sem, cnt in sg:
                t.wait_ge(sem.h, cnt)
            t.matmul(drow(4), lhsT=ones1[:, :], rhs=v_A, start=True, stop=True)
            t.matmul(drow(6), lhsT=ones1[:, :], rhs=v_A2, start=True,
                     stop=True).then_inc(s_st.h, 1)
        s_st.inc()
        @blk.scalar
        def _(c, st=s_st.n, rm=s_rm.n):
            c.wait_ge(s_st.h, st)
            c.wait_ge(s_rm.h, rm)
            c.copy(out=rbc[0:1, 2 * 512:3 * 512], in_=drow(4))
            c.copy(out=rbc[0:1, 3 * 512:4 * 512], in_=drow(6)).then_inc(s_ste.h, 1)
        s_ste.inc()
        for i in range(2):
            slot_guard[i] = (s_ste, s_ste.n)
        slot_state[0] = 0
        @blk.gpsimd
        def _(g, ste=s_ste.n, gn=s_gn.n):
            g.wait_ge(s_ste.h, ste)
            if s_rm.n >= 1:
                g.wait_ge(s_gn.h, gn)
            g.tensor_scalar(out=rbc[0:1, 4 * 512:5 * 512], in0=rbc[0:1, 2 * 512:3 * 512],
                            scalar1=1.0 / HD, scalar2=None, op0=ALU.mult)
            ins = g.tensor_scalar(out=rbc[0:1, 5 * 512:6 * 512], in0=rbc[0:1, 3 * 512:4 * 512],
                                  scalar1=1.0 / HD, scalar2=None, op0=ALU.mult)
            hop(g, s_gh, ins)
            ins = g.tensor_tensor(out=rbc[0:1, 6 * 512:7 * 512], in0=rbc[0:1, 4 * 512:5 * 512],
                                  in1=rbc[0:1, 4 * 512:5 * 512], op=ALU.mult)
            hop(g, s_gh, ins)
            g.tensor_tensor(out=rbc[0:1, 6 * 512:7 * 512], in0=rbc[0:1, 5 * 512:6 * 512],
                            in1=rbc[0:1, 6 * 512:7 * 512], op=ALU.subtract).then_inc(s_rm.h, 1)
        s_rm.inc()
        @blk.scalar
        def _(c, rm=s_rm.n):
            c.wait_ge(s_rm.h, rm)
            ins = c.activation(out=rbc[0:1, 7 * 512:8 * 512], in_=rbc[0:1, 6 * 512:7 * 512],
                               func=AF.Ln, bias=epsb[0:1, 0:1])
            hop(c, s_ch2, ins)
            c.activation(out=rbc[0:1, 2 * 512:3 * 512], in_=rbc[0:1, 7 * 512:8 * 512],
                         func=AF.Exp, scale=-0.5).then_inc(s_rv.h, 1)
        s_rv.inc()
        @blk.gpsimd
        def _(g, rv=s_rv.n, gnp=s_gn.n):
            g.wait_ge(s_rv.h, rv)
            g.wait_ge(s_gn.h, gnp)
            ins = g.tensor_tensor(out=rbc[0:1, 3 * 512:4 * 512], in0=rbc[0:1, 4 * 512:5 * 512],
                                  in1=rbc[0:1, 2 * 512:3 * 512], op=ALU.mult)
            hop(g, s_gh, ins)
            g.partition_broadcast(rbc[:, 2 * 512:4 * 512], rbc[0:1, 2 * 512:4 * 512]).then_inc(s_rm.h, 1)
        s_rm.inc()
        @blk.vector
        def _(v, rm=s_rm.n, hh_=hh_, gnd=s_gnd.n):
            v.wait_ge(s_rm.h, rm)
            if s_gn.n >= 1:
                v.wait_ge(s_gnd.h, gnd)
            ins = v.tensor_tensor(out=sct512A, in0=v_A, in1=rbc[:, 2 * 512:3 * 512], op=ALU.mult)
            hop(v, s_vh, ins)
            ins = v.tensor_tensor(out=sct512B, in0=sct512A,
                                  in1=rbc[:, 3 * 512:4 * 512], op=ALU.subtract)
            hop(v, s_vh, ins)
            v.tensor_scalar(out=v_gn, in0=sct512B, scalar1=gbv[:, hh_:hh_ + 1],
                            scalar2=gbv[:, 4 + hh_:5 + hh_], op0=ALU.mult,
                            op1=ALU.add).then_inc(s_gn.h, 1)
        s_gn.inc()
        @blk.sync
        def _(sy, gn=s_gn.n, hh_=hh_, qc=qc):
            sy.wait_ge(s_gn.h, gn)
            sy.dma_start(out=d_agn_i[hh_ * 128:(hh_ + 1) * 128, qc * TC:(qc + 1) * TC],
                         in_=v_gn).then_inc(s_gnd.h, 16)
        s_gnd.inc(16)
        if K_NOGATHER:
            return
        it = hh_ * QC + qc
        want = n_gblk if K_GATHER_FIRST else (it + 1) * n_gblk // (HPC * QC)
        while st2["gb"] < min(want, n_gblk):
            emit_gather_block(st2["gb"])
            st2["gb"] += 1

    pend = None
    for hh_ in range(HPC):
        for qt in range(NTT):
            if hh_ * NTT + qt >= K_QTLIM:
                break
            mx_n = emit_scores(hh_, qt)
            if pend is not None:
                emit_post(*pend)
            emit_exp(hh_, qt, mx_n)
            pend = (hh_, qt)
    if pend is not None:
        emit_post(*pend)
    while (not K_NOGATHER) and st2["gb"] < n_gblk:
        emit_gather_block(st2["gb"])
        st2["gb"] += 1
    GND_ALL = s_gnd.n

    @blk.gpsimd
    def _(g, gnd=GND_ALL):
        g.wait_ge(s_gnd.h, gnd)
        g.collective_compute(
            "AllGather", ALU.bypass,
            replica_groups=[[0, 1, 2, 3], [4, 5, 6, 7]],
            ins=[d_agn_i.ap().opt()], outs=[d_agn_o.ap().opt()],
        ).then_inc(s_ccn.h, 1)
    s_ccn.inc()

    if "gn" in taps:
        @blk.sync
        def _(sy, gnd=GND_ALL):
            sy.wait_ge(s_gnd.h, gnd)
            sy.dma_start(out=tap_dram["gn"], in_=d_agn_i.ap()).then_inc(s_out.h, 16)
        s_out.inc(16)
    if "wg" in taps:
        @blk.sync
        def _(sy):
            sy.wait_ge(s_ccg.h, 1)
            sy.dma_start(out=tap_dram["wg"], in_=d_agg_o.ap()).then_inc(s_out.h, 16)
        s_out.inc(16)

    if phases < 3:
        @blk.sync
        def _(sy, n=s_out.n, gnd=GND_ALL, cmbn=s_cmb.n):
            if not K_NOGATHER:
                sy.wait_ge(s_ccg.h, 1)
                sy.wait_ge(s_ccu.h, 1)
                sy.wait_ge(s_ccd.h, 1)
            sy.wait_ge(s_ccn.h, 1)
            sy.wait_ge(s_gnd.h, gnd)
            sy.wait_ge(s_cmb.h, cmbn)
            sy.wait_ge(s_out.h, n)
        _replay()
        es.close()
        return nc, tap_dram

    GSEL_ALL = s_gsel.n
    GTE_ALL = s_gte.n
    GN_ALL = s_gn.n
    RM_ALL = s_rm.n
    for tg in range(TOWN // 128):
        @blk.sync
        def _(sy, tg=tg, lnv=s_ln.n, sqv=s_sq.n, ga=GSEL_ALL):
            if tg == 0:
                sy.wait_ge(s_gsel.h, ga)
            else:
                sy.wait_ge(s_ln.h, lnv)
                sy.wait_ge(s_sq.h, sqv)
            sy.dma_start(out=v_x[:, :], in_=d_xo[tg * 128:(tg + 1) * 128, :]).then_inc(s_xo.h, 16)
        s_xo.inc(16)
        xv = s_xo.n
        @blk.scalar
        def _(c, xv=xv, tg=tg, gna=GN_ALL):
            c.wait_ge(s_xo.h, xv)
            if tg == 0:
                c.wait_ge(s_gn.h, gna)
            c.activation(out=v_scr2, in_=v_x[:, :], func=AF.Square,
                         accum_out=csc[:, 1:2]).then_inc(s_sq.h, 1)
        s_sq.inc()
        @blk.vector
        def _(v, xv=xv, sq=s_sq.n, tg=tg, gna=GN_ALL, rma=RM_ALL):
            v.wait_ge(s_xo.h, xv)
            if tg == 0:
                v.wait_ge(s_gn.h, gna)
                v.wait_ge(s_rm.h, rma)
            ins = v.tensor_scalar(out=v_scr1, in0=v_x[:, :], scalar1=1.0, scalar2=None,
                                  op0=ALU.mult, op1=ALU.add, accum_out=csc[:, 0:1])
            hop(v, s_vh, ins)
            v.wait_ge(s_sq.h, sq)
            v.tensor_scalar(out=csc[:, 2:3], in0=csc[:, 0:1], scalar1=1.0 / D,
                            scalar2=None, op0=ALU.mult)
            ins = v.tensor_scalar(out=csc[:, 3:4], in0=csc[:, 1:2], scalar1=1.0 / D,
                                  scalar2=None, op0=ALU.mult)
            hop(v, s_vh, ins)
            ins = v.tensor_tensor(out=csc[:, 4:5], in0=csc[:, 2:3], in1=csc[:, 2:3], op=ALU.mult)
            hop(v, s_vh, ins)
            v.tensor_tensor(out=csc[:, 6:7], in0=csc[:, 3:4], in1=csc[:, 4:5],
                            op=ALU.subtract).then_inc(s_rv.h, 1)
        s_rv.inc()
        @blk.scalar
        def _(c, rv=s_rv.n):
            c.wait_ge(s_rv.h, rv)
            ins = c.activation(out=csc[:, 7:8], in_=csc[:, 6:7], func=AF.Ln,
                               bias=epsb[:, 0:1])
            hop(c, s_ch2, ins)
            c.activation(out=csc[:, 8:9], in_=csc[:, 7:8], func=AF.Exp,
                         scale=-0.5).then_inc(s_rv.h, 1)
        s_rv.inc()
        @blk.vector
        def _(v, rv=s_rv.n, tg=tg, tpv=s_tp.n):
            v.wait_ge(s_rv.h, rv)
            ins = v.scalar_tensor_tensor(out=csc[:, 10:11], in0=csc[:, 8:9],
                                         scalar=-1.0, in1=csc[:, 2:3],
                                         op0=ALU.mult, op1=ALU.mult)
            hop(v, s_vh, ins)
            if tg >= 1:
                v.wait_ge(s_tp.h, tpv)
            v.tensor_scalar(out=v_xntok, in0=v_x[:, :], scalar1=csc[:, 8:9],
                            scalar2=csc[:, 10:11], op0=ALU.mult, op1=ALU.add).then_inc(s_ln.h, 1)
        s_ln.inc()
        @blk.tensor
        def _(t, lnv=s_ln.n, tg=tg, xfv=s_xf.n, avd=s_avd.n, cmbv=s_cmb.n,
              rrv=s_rr.n, stev=s_ste.n, gtev=s_gte.n, stv=s_st.n, rmv=s_rm.n):
            t.wait_ge(s_ln.h, lnv)
            if tg == 0:
                t.wait_ge(s_avd.h, avd)
                t.wait_ge(s_cmb.h, cmbv)
                t.wait_ge(s_rr.h, rrv)
                t.wait_ge(s_ste.h, stev)
                t.wait_ge(s_st.h, stv)
                t.wait_ge(s_rm.h, rmv)
                t.wait_ge(s_gte.h, gtev)
            if tg >= 1:
                t.wait_ge(s_xf.h, xfv)
            for ci in range(NCI):
                ins = t.transpose(bankb(4 + ci // 8)[:, (ci % 8) * 128:(ci % 8 + 1) * 128],
                                  in_=v_xntok[:, ci * 128:(ci + 1) * 128],
                                  identity=ident[:, :])
            ins.then_inc(s_tp.h, 1)
        s_tp.inc()
        @blk.scalar
        def _(c, tv=s_tp.n, tg=tg):
            c.wait_ge(s_tp.h, tv)
            for half in range(2):
                src = bankb(4 + half).rearrange("p (s q) -> p s q", q=128)
                dst = v_xnown.rearrange("p (ci t) -> p ci t", t=TOWN)[
                    :, 8 * half:8 * half + 8, tg * 128:(tg + 1) * 128]
                c.copy(out=dst, in_=src).then_inc(s_xf.h, 1)
        s_xf.inc(2)
    XNOWN = s_xf.n

    NTCH = TOWN // TC
    assert TOWN % TC == 0 or TOWN < TC
    tchs = list(range(max(1, TOWN // TC)))
    TW = min(TC, TOWN)
    for ot in range(OT):
        bi = ot % 2
        sw = s_wgP[bi]
        @blk.sync
        def _(sy, ot=ot, bi=bi, fm=s_fmm.n, gnd=GND_ALL, sw=sw, avdf=s_avd.n):
            sy.wait_ge(s_ccg.h, 1)
            if ot == 0:
                sy.wait_ge(s_gnd.h, gnd)
                sy.wait_ge(s_avd.h, avdf)
            if ot >= 2:
                sy.wait_ge(s_fmm.h, fm)
            src = d_agg_o.ap().rearrange("(ci ot) p q -> ci ot p q", ot=OT)[:, ot].rearrange(
                "ci p q -> p ci q")
            sy.dma_start(out=v_wg[bi].rearrange("p (ci q) -> p ci q", q=128),
                         in_=src).then_inc(sw.h, 16)
        @blk.scalar
        def _(c, ot=ot, bi=bi, fm=s_fmm.n, gnd=GND_ALL, sw=sw, avdf=s_avd.n):
            c.wait_ge(s_ccu.h, 1)
            c.wait_ge(s_ccd.h, 1)
            if ot == 0:
                c.wait_ge(s_gnd.h, gnd)
                c.wait_ge(s_avd.h, avdf)
            if ot >= 2:
                c.wait_ge(s_fmm.h, fm)
            src = d_agu_o.ap().rearrange("(ci ot) p q -> ci ot p q", ot=OT)[:, ot].rearrange(
                "ci p q -> p ci q")
            c.dma_start(out=v_wu[bi].rearrange("p (ci q) -> p ci q", q=128),
                        in_=src).then_inc(sw.h, 16)
        sw.inc(32)
        wv = sw.n
        @blk.tensor
        def _(t, ot=ot, bi=bi, wv=wv, xn=XNOWN, hv=s_h.n, tchs=tchs, TW=TW, sw=sw):
            t.wait_ge(sw.h, wv)
            t.wait_ge(s_xf.h, xn)
            if ot >= 2:
                t.wait_ge(s_h.h, hv)
            grp = (ot % 2) * 4
            for ci in range(NCI):
                st_ = (ci == 0)
                sp_ = (ci == NCI - 1)
                for tch in tchs:
                    t.matmul(bank(grp + tch, TW), lhsT=v_wg[bi][:, ci * 128:(ci + 1) * 128],
                             rhs=v_xnown[:, ci * TOWN + tch * TW:ci * TOWN + (tch + 1) * TW],
                             start=st_, stop=sp_)
                    ins = t.matmul(bank(grp + 2 + tch, TW), lhsT=v_wu[bi][:, ci * 128:(ci + 1) * 128],
                                   rhs=v_xnown[:, ci * TOWN + tch * TW:ci * TOWN + (tch + 1) * TW],
                                   start=st_, stop=sp_)
            ins.then_inc(s_fmm.h, 1)
        s_fmm.inc()
        @blk.scalar
        def _(c, ot=ot, fm=s_fmm.n, hv=s_h.n, tchs=tchs, TW=TW):
            c.wait_ge(s_fmm.h, fm)
            if ot >= 1:
                c.wait_ge(s_h.h, hv)
            grp = (ot % 2) * 4
            for tch in tchs:
                ins = c.activation(out=(v_gt0 if tch == 0 else v_gt1)[:, 0:TW],
                                   in_=bank(grp + tch, TW), func=AF.Silu)
            ins.then_inc(s_si.h, 1)
        s_si.inc()
        @blk.vector
        def _(v, ot=ot, si=s_si.n, tchs=tchs, TW=TW):
            v.wait_ge(s_si.h, si)
            grp = (ot % 2) * 4
            for tch in tchs:
                ins = v.tensor_tensor(out=h_sl(ot, tch * TW, TW), in0=bank(grp + 2 + tch, TW),
                                      in1=(v_gt0 if tch == 0 else v_gt1)[:, 0:TW], op=ALU.mult)
            ins.then_inc(s_h.h, 1)
        s_h.inc()
    H_DONE = s_h.n

    if "h" in taps:
        @blk.sync
        def _(sy, hv=H_DONE):
            sy.wait_ge(s_h.h, hv)
            for ot in range(OT):
                sy.dma_start(out=tap_dram["h"][:, ot * TOWN:(ot + 1) * TOWN],
                             in_=h_sl(ot, 0, TOWN)).then_inc(s_out.h, 16)
        s_out.inc(16 * OT)

    if phases < 4:
        @blk.sync
        def _(sy, n=s_out.n, hv=H_DONE):
            sy.wait_ge(s_h.h, hv)
            sy.wait_ge(s_ccn.h, 1)
            sy.wait_ge(s_out.h, n)
        _replay()
        es.close()
        return nc, tap_dram

    @blk.gpsimd
    def _(g, hv=H_DONE, avd=s_avd.n):
        g.wait_ge(s_ccn.h, 1)
        g.wait_ge(s_avd.h, avd)
        g.wait_ge(s_h.h, hv)
        with g.register("rk") as rk:
            g.cc_rank_ld(rk, replica_groups=[[0, 1, 2, 3], [4, 5, 6, 7]])
            g.reg_mul(rk, rk, TOWN)
            off = g.snap(rk)
            src = d_agn_o.rearrange("(fc p) t -> p fc t", p=128)[:, :, bass.ds(off, TOWN)]
            g.dma_start(out=v_gnall.rearrange("p (fc t) -> p fc t", t=TOWN),
                        in_=src).then_inc(s_gna.h, 16)
    s_gna.inc(16)
    GNA = s_gna.n

    for et in range(NCI):
        bi = et % 2
        sw = s_w4P[bi]
        @blk.sync
        def _(sy, et=et, bi=bi, fe=s_fe.n, hv=H_DONE, sw=sw, fmp=s_fmm.n,
              avdf=s_avd.n):
            if et == 0:
                sy.wait_ge(s_h.h, hv)
                sy.wait_ge(s_fmm.h, fmp)
                sy.wait_ge(s_avd.h, avdf)
            if et >= 2:
                sy.wait_ge(s_fe.h, fe)
            src = d_wot.rearrange("(fc e) p q -> fc e p q", e=NCI)[:, et].rearrange(
                "fc p q -> p fc q")
            sy.dma_start(out=v_wot[bi].rearrange("p (fc q) -> p fc q", q=128),
                         in_=src).then_inc(sw.h, 16)
        @blk.scalar
        def _(c, et=et, bi=bi, fe=s_fe.n, hv=H_DONE, sw=sw, fmp=s_fmm.n,
              avdf=s_avd.n):
            if et == 0:
                c.wait_ge(s_h.h, hv)
                c.wait_ge(s_fmm.h, fmp)
                c.wait_ge(s_avd.h, avdf)
            if et >= 2:
                c.wait_ge(s_fe.h, fe)
            src = d_agd_o.ap().rearrange("(oc e) p q -> oc e p q", e=NCI)[0:OT, et].rearrange(
                "oc p q -> p oc q")
            c.dma_start(out=v_wdt[bi].rearrange("p (oc q) -> p oc q", q=128),
                        in_=src).then_inc(sw.h, 16)
        sw.inc(32)
        wv = sw.n
        for tch in tchs:
            bk_ = tch % 2
            @blk.tensor
            def _(t, et=et, bi=bi, tch=tch, bk_=bk_, wv=wv, gna=GNA, hv=H_DONE,
                  fe=s_fe.n, TW=TW, sw=sw):
                t.wait_ge(sw.h, wv)
                t.wait_ge(s_gna.h, gna)
                t.wait_ge(s_h.h, hv)
                t.wait_ge(s_fe.h, fe)
                for fc in range(NCI):
                    t.matmul(bank(bk_, TW), lhsT=v_wot[bi][:, fc * 128:(fc + 1) * 128],
                             rhs=v_gnall[:, fc * TOWN + tch * TW:fc * TOWN + (tch + 1) * TW],
                             start=(fc == 0), stop=False)
                for oc in range(OT):
                    ins = t.matmul(bank(bk_, TW), lhsT=v_wdt[bi][:, oc * 128:(oc + 1) * 128],
                                   rhs=h_sl(oc, tch * TW, TW),
                                   start=False, stop=(oc == OT - 1))
                ins.then_inc(s_fmm.h, 1)
            s_fmm.inc()
            @blk.scalar
            def _(c, fm=s_fmm.n, tch=tch, bk_=bk_, ftp=s_ftp.n, TW=TW):
                c.wait_ge(s_fmm.h, fm)
                c.wait_ge(s_ftp.h, ftp)
                c.copy(out=fmt[:, tch * TC:tch * TC + TW], in_=bank(bk_, TW)).then_inc(s_fe.h, 1)
            s_fe.inc()
            @blk.tensor
            def _(t, fe=s_fe.n, tch=tch, fte=s_fte.n, TW=TW):
                t.wait_ge(s_fe.h, fe)
                t.wait_ge(s_fte.h, fte)
                for tsub in range(TW // 128):
                    ins = t.transpose(bankb(2)[:, tsub * 128:(tsub + 1) * 128],
                                      in_=fmt[:, tch * TC + tsub * 128:tch * TC + (tsub + 1) * 128],
                                      identity=ident[:, :])
                ins.then_inc(s_ftp.h, 1)
            s_ftp.inc()
            @blk.scalar
            def _(c, tv=s_ftp.n, et=et, tch=tch, TW=TW):
                c.wait_ge(s_ftp.h, tv)
                nts = TW // 128
                src = bankb(2)[:, 0:nts * 128].rearrange("p (s q) -> p s q", q=128)
                dst = v_outr.rearrange("p (tt e) -> p tt e", e=D)[
                    :, tch * (TC // 128):tch * (TC // 128) + nts, et * 128:(et + 1) * 128]
                c.copy(out=dst, in_=src).then_inc(s_fte.h, 1)
            s_fte.inc()
    FTE = s_fte.n

    for tg in range(TOWN // 128):
        @blk.sync
        def _(sy, tg=tg, fte=FTE, oa=s_oadd.n, fmp=s_fmm.n):
            sy.wait_ge(s_fte.h, fte)
            sy.wait_ge(s_fmm.h, fmp)
            if tg >= 1:
                sy.wait_ge(s_oadd.h, oa)
            dst = big1[:, (tg % 2) * 8192:(tg % 2) * 8192 + 4096].bitcast(F32)
            sy.dma_start(out=dst, in_=d_xo[tg * 128:(tg + 1) * 128, :]).then_inc(s_xl.h, 16)
        s_xl.inc(16)
        @blk.vector
        def _(v, tg=tg, xl=s_xl.n, ov=s_out.n):
            v.wait_ge(s_xl.h, xl)
            if tg >= 2:
                v.wait_ge(s_out.h, ov)
            xt = big1[:, (tg % 2) * 8192:(tg % 2) * 8192 + 4096].bitcast(F32)
            ot_ = big1[:, 16384 + (tg % 2) * 4096:16384 + (tg % 2) * 4096 + 4096].bitcast(F32)
            v.tensor_tensor(out=ot_, in0=v_outr[:, tg * D:(tg + 1) * D],
                            in1=xt, op=ALU.add).then_inc(s_oadd.h, 1)
        s_oadd.inc()
        @blk.sync
        def _(sy, tg=tg, oa=s_oadd.n):
            sy.wait_ge(s_oadd.h, oa)
            src = big1[:, 16384 + (tg % 2) * 4096:16384 + (tg % 2) * 4096 + 4096].bitcast(F32)
            sy.dma_start(out=d_out[tg * 128:(tg + 1) * 128, :], in_=src).then_inc(s_out.h, 16)
        s_out.inc(16)

    if "gnall" in taps:
        @blk.sync
        def _(sy, gna=GNA):
            sy.wait_ge(s_gna.h, gna)
            sy.dma_start(out=tap_dram["gnall"], in_=v_gnall).then_inc(s_out.h, 16)
        s_out.inc(16)

    @blk.sync
    def _(sy, n=s_out.n):
        sy.wait_ge(s_out.h, n)

    _replay()
    es.close()
    return nc, tap_dram


def prep_inputs(geo, L0_attn_x, L0_attn_W_qkv, L0_attn_W_o, L0_attn_lambda_q1,
                L0_attn_lambda_k1, L0_attn_lambda_q2, L0_attn_lambda_k2,
                L0_attn_gn_gamma, L0_attn_gn_beta, L0_ffn_lut, L0_ffn_gate_idx,
                L0_ffn_up_idx, L0_ffn_down_idx):
    g_ = geo
    D, HD, HPC, TOWN, CSH, DSH = g_.D, g_.HD, g_.HPC, g_.TOWN, g_.CSH, g_.DSH
    x = np.ascontiguousarray(np.asarray(L0_attn_x, dtype=np.float32))
    wqkv = np.asarray(L0_attn_W_qkv, dtype=np.float32)
    wo = np.ascontiguousarray(np.asarray(L0_attn_W_o, dtype=np.float32))
    lam4 = np.stack([np.asarray(a, dtype=np.float32).reshape(HD) for a in
                     (L0_attn_lambda_q1, L0_attn_lambda_k1,
                      L0_attn_lambda_q2, L0_attn_lambda_k2)], axis=0)
    gamma = np.asarray(L0_attn_gn_gamma, dtype=np.float32)
    beta = np.asarray(L0_attn_gn_beta, dtype=np.float32)
    lut = np.asarray(L0_ffn_lut, dtype=np.float32).reshape(1, 16)
    gidx = np.asarray(L0_ffn_gate_idx, dtype=np.int32)
    uidx = np.asarray(L0_ffn_up_idx, dtype=np.int32)
    didx = np.asarray(L0_ffn_down_idx, dtype=np.int32)
    didx_pad = np.zeros((D, NCORE * DSH), dtype=np.int32)
    didx_pad[:, :g_.INTER] = didx

    in_maps = []
    for core in range(NCORE):
        b = core // 4
        u = core % 4
        wq = wqkv[0 * D + HPC * u * HD: 0 * D + HPC * (u + 1) * HD]
        wk = wqkv[1 * D + HPC * u * HD: 1 * D + HPC * (u + 1) * HD]
        wv = wqkv[2 * D + HPC * u * HD: 2 * D + HPC * (u + 1) * HD]
        im = {
            "x": np.ascontiguousarray(x[b]),
            "xo": np.ascontiguousarray(x[b, u * TOWN:(u + 1) * TOWN]),
            "wqkv": np.ascontiguousarray(np.concatenate([wq, wk, wv], axis=0)),
            "wo": wo,
            "lam4": np.ascontiguousarray(lam4),
            "gnw": np.ascontiguousarray(
                np.concatenate([gamma[HPC * u:HPC * (u + 1)],
                                beta[HPC * u:HPC * (u + 1)]], axis=0)),
            "lut": np.ascontiguousarray(lut),
            "gidx": np.ascontiguousarray(gidx[:, CSH * core:CSH * (core + 1)]),
            "uidx": np.ascontiguousarray(uidx[:, CSH * core:CSH * (core + 1)]),
            "didx": np.ascontiguousarray(didx_pad[:, DSH * core:DSH * (core + 1)]),
        }
        in_maps.append(im)
    return in_maps


def kernel(**inputs):
    geo = Geo(mini=False)
    in_maps = prep_inputs(geo, **inputs)
    nc, _ = build(geo)
    lower_extended_insts(nc)
    res = run_bass_kernel_spmd(nc, in_maps, core_ids=list(range(NCORE)))
    out = np.empty((geo.B, geo.S, geo.D), dtype=np.float32)
    for core in range(NCORE):
        b = core // 4
        u = core % 4
        out[b, u * geo.TOWN:(u + 1) * geo.TOWN] = \
            res.results[core]["out"].reshape(geo.TOWN, geo.D)
    return out


# version 14 (final)
# speedup vs baseline: 1.0334x; 1.0334x over previous
import math
import numpy as np
from contextlib import ExitStack

import concourse.bass as bass
import concourse.mybir as mybir
from concourse.bass_utils import run_bass_kernel_spmd
from concourse import library_config
from concourse.bass_isa import ReduceOp
from concourse.library_overlay import lower_extended_insts

F32 = mybir.dt.float32
BF16 = mybir.dt.bfloat16
F16 = mybir.dt.float16
I32 = mybir.dt.int32
ALU = mybir.AluOpType
AXL = mybir.AxisListType
AF = mybir.ActivationFunctionType

NCORE = 8
K_GATHER_FIRST = False
K_GDBG = False
K_NOGATHER = False
K_QTLIM = 10 ** 9


class Geo:

    def __init__(self, mini=False):
        self.B = 2
        self.D = 2048
        self.H = 16
        self.HPC = 4
        self.HD = 128
        self.NCI = self.D // 128
        self.CSH = self.D // NCORE
        self.TC = 512
        if mini:
            self.S = 1024
            self.INTER = 1024
            self.DOT = 1
        else:
            self.S = 4096
            self.INTER = 5504
            self.DOT = 6
        self.OT = self.INTER // 128
        self.DSH = self.DOT * 128
        self.TOWN = self.S // 4
        self.NT = self.S // self.TC
        self.QC = self.S // self.TC
        self.NTT = self.S // 128
        self.GUB = (self.OT + 3) // 4

    LN_EPS = 1e-5
    LAMBDA_INIT = 0.8 - 0.6 * math.exp(-0.3 * 12)
    SCALE = 0.125


def build(geo, phases=4, taps=()):
    g_ = geo
    B, S, D = g_.B, g_.S, g_.D
    HPC, HD, NCI, TC = g_.HPC, g_.HD, g_.NCI, g_.TC
    INTER, OT, DOT, DSH, CSH = g_.INTER, g_.OT, g_.DOT, g_.DSH, g_.CSH
    TOWN, NT, QC, NTT, GUB = g_.TOWN, g_.NT, g_.QC, g_.NTT, g_.GUB
    LN_EPS, LAMBDA_INIT, SCALE = g_.LN_EPS, g_.LAMBDA_INIT, g_.SCALE

    nc = bass.Bass("TRN2", target_bir_lowering=False, debug=False)

    d_x = nc.dram_tensor("x", [S, D], F32, kind="ExternalInput").ap()
    d_xo = nc.dram_tensor("xo", [TOWN, D], F32, kind="ExternalInput").ap()
    d_wqkv = nc.dram_tensor("wqkv", [HPC * 3 * HD, D], F32, kind="ExternalInput").ap()
    d_wo = nc.dram_tensor("wo", [D, D], F32, kind="ExternalInput").ap()
    d_lam4 = nc.dram_tensor("lam4", [4, HD], F32, kind="ExternalInput").ap()
    d_gnw = nc.dram_tensor("gnw", [8, HD], F32, kind="ExternalInput").ap()
    d_lut = nc.dram_tensor("lut", [1, 16], F32, kind="ExternalInput").ap()
    d_gidx = nc.dram_tensor("gidx", [INTER, CSH], I32, kind="ExternalInput").ap()
    d_uidx = nc.dram_tensor("uidx", [INTER, CSH], I32, kind="ExternalInput").ap()
    d_didx = nc.dram_tensor("didx", [D, DSH], I32, kind="ExternalInput").ap()
    d_out = nc.dram_tensor("out", [TOWN, D], F32, kind="ExternalOutput").ap()

    d_wot = nc.dram_tensor("wot_t", [NCI * NCI, 128, 128], BF16)
    d_agg_i = nc.dram_tensor("agg_i", [2 * OT, 128, 128], BF16)
    d_agg_o = nc.dram_tensor("agg_o", [NCORE * 2 * OT, 128, 128], BF16,
                             addr_space="Shared")
    d_agu_i = nc.dram_tensor("agu_i", [2 * OT, 128, 128], BF16)
    d_agu_o = nc.dram_tensor("agu_o", [NCORE * 2 * OT, 128, 128], BF16,
                             addr_space="Shared")
    d_agd_i = nc.dram_tensor("agd_i", [DOT * NCI, 128, 128], BF16)
    d_agd_o = nc.dram_tensor("agd_o", [NCORE * DOT * NCI, 128, 128], BF16,
                             addr_space="Shared")
    d_agn_i = nc.dram_tensor("agn_i", [HPC * HD, S], BF16)
    d_agn_o = nc.dram_tensor("agn_o", [4 * HPC * HD, S], BF16)

    tap_dram = {}
    def mktap(name, shape, dt=F32):
        if name in taps:
            tap_dram[name] = nc.dram_tensor("tap_" + name, shape, dt,
                                            kind="ExternalOutput").ap()
    mktap("qst", [128, HPC * S], BF16)
    mktap("kst", [128, HPC * S], BF16)
    mktap("vst", [128, NTT * 512], BF16)
    mktap("cbias", [128, 8], F32)
    mktap("gn", [HPC * HD, S], BF16)
    mktap("h", [128, OT * TOWN], BF16)
    mktap("gnall", [128, NCI * TOWN], BF16)
    mktap("wg", [NCORE * 2 * OT, 128, 128], BF16)

    es = ExitStack()
    SB = lambda name, shape, dt: es.enter_context(nc.sbuf_tensor(name, shape, dt))

    H_A = min(24, OT)
    big1 = SB("big1", [128, max(NCI * 1536, H_A * TOWN, 24576 if S == 4096 else 0)], BF16)
    big2_sz = max(18944, (OT - H_A) * TOWN if OT > H_A else 0) + 1024
    big2 = SB("big2", [128, big2_sz], BF16)
    qst = SB("qst", [128, max(HPC * S, NCI * TOWN)], BF16)
    kst = SB("kst", [128, max(HPC * S, 8 * D)], BF16)
    vst = SB("vst", [128, max(NTT * 512, NCI * TOWN, 2 * D + 2 * OT * 128 + 1024)], BF16)
    ident = SB("ident", [128, 128], BF16)
    identf = SB("identf", [128, 128], F32)
    dmn = SB("dmn", [128, 128], F32)
    lutc = SB("lutc", [128, 16], F32)
    lamneg = SB("lamneg", [128, 1], F32)
    gbv = SB("gbv", [128, 8], F32)
    cbias = SB("cbias", [128, 8], F32)
    ones1 = SB("ones1", [128, 1], BF16)
    nmaxb = SB("nmaxb", [128, 16], F32)
    csc = SB("csc", [128, 16], F32)
    rbc = SB("rbc", [128, 8 * 512], F32)
    smtmp = SB("smtmp", [128, 80], F32)
    epsb = SB("epsb", [128, 1], F32)
    iot = rbc[:, 0:TC].bitcast(I32)


    v_x = big2[:, 0:4096].bitcast(F32)
    v_wraw = [big2[:, 0:2048], big2[:, 2048:4096]]
    v_xntok = big2[:, 4096:6144]
    v_wotst = big2[:, 4096:6144]
    v_xnfm = big2[:, 6144:6144 + NCI * TC]
    v_scr1 = big2[:, 14336:16384]
    v_scr2 = big2[:, 16384:18432]
    v_sq = big2[:, 6144 + NCI * TC:6144 + NCI * TC + S]
    g_idx = big2[:, 0:4096].bitcast(I32)
    g_gg = big2[:, 4096:6144]
    g_w = big2[:, 6144:8192]
    s1b = big1[:, 0:4096].bitcast(F16)
    s2b = big1[:, 4096:8192].bitcast(F16)
    p1b = big1[:, 8192:12288]
    p2b = big1[:, 12288:16384]
    pt1 = big1[:, 16384:20480]
    pt2 = big1[:, 20480:24576]
    g_stA = big2[:, 8192:9216]
    g_stB = big2[:, 9216:10240]
    v_A = big2[:, 10240:10752]
    v_A2 = big2[:, 10752:11264]
    v_gn = big2[:, 11264:11776]
    g_g2s = big2[:, 11776:13824]
    R1bc = big2[:, 13824:14848].bitcast(F32)
    R2bc = big2[:, 14848:15872].bitcast(F32)
    sct512A = big2[:, 15872:16896].bitcast(F32)
    sct512B = big2[:, 16896:17920].bitcast(F32)
    rr = big2[0:1, 18432:18944].bitcast(F32)
    def h_sl(ot, lo, n):
        if ot < H_A:
            return big1[:, ot * TOWN + lo: ot * TOWN + lo + n]
        return big2[:, (ot - H_A) * TOWN + lo: (ot - H_A) * TOWN + lo + n]
    v_gt0 = big2[:, big2_sz - 1024:big2_sz - 512]
    v_gt1 = big2[:, big2_sz - 512:big2_sz]
    v_wg = [kst[:, 0:2048], kst[:, 2048:4096]]
    v_wu = [kst[:, 4096:6144], kst[:, 6144:8192]]
    v_xnown = vst[:, 0:NCI * TOWN]
    v_wot = [vst[:, 0:2048], vst[:, 2048:4096]]
    v_wdt = [vst[:, 4096:4096 + OT * 128], vst[:, 4096 + OT * 128:4096 + 2 * OT * 128]]
    v_gnall = qst[:, 0:NCI * TOWN]
    fmt = vst[:, 2 * D + 2 * OT * 128:2 * D + 2 * OT * 128 + 1024]
    v_outr = kst[:, 0:(TOWN // 128) * D]

    ps = es.enter_context(nc.psum_tensor("ps", [128, 4096], F32))
    psb = ps.ap().bitcast(BF16)
    def bank(i, w=512):
        return ps[:, i * 512:i * 512 + w]
    def bankb(i):
        return psb[:, i * 1024:(i + 1) * 1024]
    def drow(i):
        return ps[0:1, i * 512:(i + 1) * 512]

    class SemC:
        def __init__(self, h):
            self.h = h
            self.n = 0
        def inc(self, k=1):
            self.n += k
            return self.n
    names = ("small wraw wt wte wsd pre x sq ln tp xf mm qe nsq nmm nmx cb "
             "sc exp msk avd rr bc cmb a2 st ste rm rv gn gnd idx gsel gtp "
             "gte gsd cc wg fmm si h xo gna w4 fe ftp fte xl oadd out "
             "vh ch2 gh wraw2 wg2 w42 ccg ccu ccd ccn se mx ptp pte dtp rre").split()
    sm = {}
    for n_ in names:
        sm[n_] = SemC(es.enter_context(nc.semaphore("s_" + n_)))
    (s_small, s_wraw, s_wt, s_wte, s_wsd, s_pre, s_x, s_sq, s_ln, s_tp, s_xf,
     s_mm, s_qe, s_nsq, s_nmm, s_nmx, s_cb, s_sc, s_exp, s_msk, s_avd, s_rr,
     s_bc, s_cmb, s_a2, s_st, s_ste, s_rm, s_rv, s_gn, s_gnd, s_idx, s_gsel,
     s_gtp, s_gte, s_gsd, s_cc, s_wg, s_fmm, s_si, s_h, s_xo, s_gna, s_w4,
     s_fe, s_ftp, s_fte, s_xl, s_oadd, s_out, s_vh, s_ch2, s_gh,
     s_wraw2, s_wg2, s_w42, s_ccg, s_ccu, s_ccd, s_ccn, s_se, s_mx,
     s_ptp, s_pte, s_dtp, s_rre) = (sm[n_] for n_ in names)
    s_wrawP = [s_wraw, s_wraw2]
    s_wgP = [s_wg, s_wg2]
    s_w4P = [s_w4, s_w42]

    def hop(eng, s, ins):
        ins.then_inc(s.h, 1)
        eng.wait_ge(s.h, s.inc())

    class _DeferBlk:

        def __init__(self):
            self.qs = {"tensor": [], "vector": [], "scalar": [],
                       "gpsimd": [], "sync": []}

        def _deco(self, name):
            def d(fn):
                self.qs[name].append(fn)
            return d

        tensor = property(lambda self: self._deco("tensor"))
        vector = property(lambda self: self._deco("vector"))
        scalar = property(lambda self: self._deco("scalar"))
        gpsimd = property(lambda self: self._deco("gpsimd"))
        sync = property(lambda self: self._deco("sync"))

    def _replay():
        with nc.Block() as rblk:
            for eng in ("sync", "gpsimd", "tensor", "scalar", "vector"):
                fns = blk.qs[eng]
                deco = getattr(rblk, eng)
                for lo in range(0, len(fns), 200):
                    grp = fns[lo:lo + 200]
                    def body(e, grp=grp):
                        for fn in grp:
                            fn(e)
                    deco(body)

    blk = _DeferBlk()

    @blk.gpsimd
    def _(g):
        g.load_library(library_config.proxy)
        with nc.allow_non_contiguous_dma(reason="tiny transposed loads"):
            g.dma_start(out=smtmp[:, 0:4], in_=d_lam4.rearrange("h d -> d h")).then_inc(s_small.h, 16)
            g.dma_start(out=smtmp[:, 8:16], in_=d_gnw.rearrange("h d -> d h")).then_inc(s_small.h, 16)
        g.dma_start(out=lutc[0:1, :], in_=d_lut).then_inc(s_small.h, 16)
        g.wait_ge(s_small.h, 48)
        g.partition_broadcast(lutc[:, :], lutc[0:1, :]).then_inc(s_pre.h, 1)
        g.iota(iot, pattern=[[1, TC]], base=0, channel_multiplier=-1).then_inc(s_pre.h, 1)
    s_small.inc(48)
    s_pre.inc(2)
    PRE_G = s_pre.n

    @blk.vector
    def _(v, pg=PRE_G):
        v.wait_ge(s_pre.h, pg)
        v.tensor_scalar(out=ident[:, :], in0=iot[:, 0:128], scalar1=0, scalar2=None,
                        op0=ALU.is_equal)
        v.tensor_scalar(out=identf[:, :], in0=iot[:, 0:128], scalar1=0, scalar2=None,
                        op0=ALU.is_equal)
        v.tensor_scalar(out=dmn[:, :], in0=iot[:, 0:128], scalar1=0,
                        scalar2=-30000.0, op0=ALU.is_gt, op1=ALU.mult)
        v.tensor_tensor(out=smtmp[:, 4:5], in0=smtmp[:, 0:1], in1=smtmp[:, 1:2], op=ALU.mult)
        v.tensor_tensor(out=smtmp[:, 5:6], in0=smtmp[:, 2:3], in1=smtmp[:, 3:4], op=ALU.mult)
        v.memset(ones1[:, :], 1.0)
        v.memset(epsb[:, :], LN_EPS).then_inc(s_pre.h, 1)
    s_pre.inc()
    PRE_V1 = s_pre.n

    @blk.scalar
    def _(c, pv=PRE_V1):
        c.wait_ge(s_pre.h, pv)
        c.activation(out=smtmp[:, 6:7], in_=smtmp[:, 4:5], func=AF.Exp)
        c.activation(out=smtmp[:, 7:8], in_=smtmp[:, 5:6], func=AF.Exp).then_inc(s_pre.h, 1)
    s_pre.inc()
    PRE_C1 = s_pre.n

    @blk.vector
    def _(v, pc=PRE_C1):
        v.wait_ge(s_pre.h, pc)
        ins = v.tensor_tensor(out=lamneg[:, :], in0=smtmp[:, 7:8], in1=smtmp[:, 6:7], op=ALU.subtract)
        hop(v, s_vh, ins)
        v.tensor_scalar(out=gbv[:, :], in0=smtmp[:, 8:16], scalar1=(1.0 - LAMBDA_INIT),
                        scalar2=None, op0=ALU.mult)
        v.tensor_scalar(out=lamneg[:, :], in0=lamneg[:, :], scalar1=-LAMBDA_INIT,
                        scalar2=None, op0=ALU.add).then_inc(s_pre.h, 1)
    s_pre.inc()
    PRE_DONE = s_pre.n

    v_wqkvT = big1[:, 0:NCI * 1536]
    NWQ = HPC * 3
    for ft in range(NWQ):
        bi = ft % 2
        sw = s_wrawP[bi]
        @blk.gpsimd
        def _(g, ft=ft, bi=bi, wtv=s_wt.n, sw=sw):
            if ft >= 2:
                g.wait_ge(s_wt.h, wtv)
            g.dma_start(out=v_wraw[bi], in_=d_wqkv[ft * 128:(ft + 1) * 128, :]).then_inc(sw.h, 16)
        sw.inc(16)
        @blk.tensor
        def _(t, ft=ft, bi=bi, wv=sw.n, pv=PRE_DONE, wte=s_wte.n, sw=sw):
            t.wait_ge(s_pre.h, pv)
            t.wait_ge(sw.h, wv)
            if ft >= 1:
                t.wait_ge(s_wte.h, wte)
            ins = None
            for ci in range(NCI):
                ins = t.transpose(bankb(4 + ci // 8)[:, (ci % 8) * 128:(ci % 8 + 1) * 128],
                                  in_=v_wraw[bi][:, ci * 128:(ci + 1) * 128],
                                  identity=ident[:, :])
            ins.then_inc(s_wt.h, 1)
        s_wt.inc()
        @blk.scalar
        def _(c, ft=ft, tv=s_wt.n):
            c.wait_ge(s_wt.h, tv)
            for half in range(2):
                src = bankb(4 + half).rearrange("p (s q) -> p s q", q=128)
                dst = v_wqkvT.rearrange("p (ci f) -> p ci f", f=1536)[
                    :, 8 * half:8 * half + 8, ft * 128:(ft + 1) * 128]
                c.copy(out=dst, in_=src).then_inc(s_wte.h, 1)
        s_wte.inc(2)
    for et in range(NCI):
        bi = et % 2
        sw = s_wrawP[(NWQ + et) % 2]
        @blk.gpsimd
        def _(g, et=et, bi=bi, wtv=s_wt.n, sw=sw, nwq=NWQ):
            g.wait_ge(s_wt.h, wtv if et >= 2 else nwq)
            g.dma_start(out=v_wraw[bi], in_=d_wo[et * 128:(et + 1) * 128, :]).then_inc(sw.h, 16)
        sw.inc(16)
        @blk.tensor
        def _(t, et=et, bi=bi, wv=sw.n, wte=s_wte.n, sw=sw):
            t.wait_ge(sw.h, wv)
            t.wait_ge(s_wte.h, wte)
            ins = None
            for fc in range(NCI):
                ins = t.transpose(bankb(4 + fc // 8)[:, (fc % 8) * 128:(fc % 8 + 1) * 128],
                                  in_=v_wraw[bi][:, fc * 128:(fc + 1) * 128],
                                  identity=ident[:, :])
            ins.then_inc(s_wt.h, 1)
        s_wt.inc()
        @blk.scalar
        def _(c, et=et, tv=s_wt.n, sd=s_wsd.n):
            c.wait_ge(s_wt.h, tv)
            if et >= 1:
                c.wait_ge(s_wsd.h, sd)
            for half in range(2):
                src = bankb(4 + half).rearrange("p (s q) -> p s q", q=128)
                dst = v_wotst.rearrange("p (s q) -> p s q", q=128)[:, 8 * half:8 * half + 8, :]
                c.copy(out=dst, in_=src).then_inc(s_wte.h, 1)
        s_wte.inc(2)
        @blk.sync
        def _(sy, et=et, ev=s_wte.n):
            sy.wait_ge(s_wte.h, ev)
            src = v_wotst.rearrange("p (fc q) -> p fc q", q=128)
            dst = d_wot.rearrange("(fc et) p q -> fc et p q", et=NCI)[:, et].rearrange(
                "fc p q -> p fc q")
            sy.dma_start(out=dst, in_=src).then_inc(s_wsd.h, 16)
        s_wsd.inc(16)
    P0_WT = s_wt.n
    P0_WTE = s_wte.n
    P0_WSD = s_wsd.n

    for tcn in range(NT):
        QE_BASE = tcn * 12
        for tt4 in range(4):
            tg = tcn * 4 + tt4
            @blk.sync
            def _(sy, tg=tg, lnv=s_ln.n, sqv=s_sq.n, p0=P0_WT):
                if tg == 0:
                    sy.wait_ge(s_wt.h, p0)
                else:
                    sy.wait_ge(s_ln.h, lnv)
                    sy.wait_ge(s_sq.h, sqv)
                sy.dma_start(out=v_x[:, :], in_=d_x[tg * 128:(tg + 1) * 128, :]).then_inc(s_x.h, 16)
            s_x.inc(16)
            xv = s_x.n
            @blk.scalar
            def _(c, xv=xv, sq=s_sq.n):
                c.wait_ge(s_x.h, xv)
                c.activation(out=v_scr2, in_=v_x[:, :], func=AF.Square,
                             accum_out=csc[:, 1:2]).then_inc(s_sq.h, 1)
            s_sq.inc()
            @blk.vector
            def _(v, xv=xv, sq=s_sq.n):
                v.wait_ge(s_x.h, xv)
                ins = v.tensor_scalar(out=v_scr1, in0=v_x[:, :], scalar1=1.0, scalar2=None,
                                      op0=ALU.mult, op1=ALU.add, accum_out=csc[:, 0:1])
                hop(v, s_vh, ins)
                v.wait_ge(s_sq.h, sq)
                v.tensor_scalar(out=csc[:, 2:3], in0=csc[:, 0:1], scalar1=1.0 / D,
                                scalar2=None, op0=ALU.mult)
                ins = v.tensor_scalar(out=csc[:, 3:4], in0=csc[:, 1:2], scalar1=1.0 / D,
                                      scalar2=None, op0=ALU.mult)
                hop(v, s_vh, ins)
                ins = v.tensor_tensor(out=csc[:, 4:5], in0=csc[:, 2:3], in1=csc[:, 2:3], op=ALU.mult)
                hop(v, s_vh, ins)
                v.tensor_tensor(out=csc[:, 6:7], in0=csc[:, 3:4], in1=csc[:, 4:5],
                                op=ALU.subtract).then_inc(s_rv.h, 1)
            s_rv.inc()
            @blk.scalar
            def _(c, rv=s_rv.n):
                c.wait_ge(s_rv.h, rv)
                ins = c.activation(out=csc[:, 7:8], in_=csc[:, 6:7], func=AF.Ln,
                                   bias=epsb[:, 0:1])
                hop(c, s_ch2, ins)
                c.activation(out=csc[:, 8:9], in_=csc[:, 7:8], func=AF.Exp,
                             scale=-0.5).then_inc(s_rv.h, 1)
            s_rv.inc()
            @blk.vector
            def _(v, rv=s_rv.n, tg=tg, tpv=s_tp.n, p0d=P0_WSD):
                v.wait_ge(s_rv.h, rv)
                ins = v.scalar_tensor_tensor(out=csc[:, 10:11], in0=csc[:, 8:9],
                                             scalar=-1.0, in1=csc[:, 2:3],
                                             op0=ALU.mult, op1=ALU.mult)
                hop(v, s_vh, ins)
                if tg == 0:
                    v.wait_ge(s_wsd.h, p0d)
                else:
                    v.wait_ge(s_tp.h, tpv)
                v.tensor_scalar(out=v_xntok, in0=v_x[:, :], scalar1=csc[:, 8:9],
                                scalar2=csc[:, 10:11], op0=ALU.mult, op1=ALU.add).then_inc(s_ln.h, 1)
            s_ln.inc()
            @blk.tensor
            def _(t, lnv=s_ln.n, tt4=tt4, tg=tg, mmv=s_mm.n, xfv=s_xf.n, p0e=P0_WTE):
                t.wait_ge(s_ln.h, lnv)
                if tg == 0:
                    t.wait_ge(s_wte.h, p0e)
                if tt4 == 0 and tg >= 4:
                    t.wait_ge(s_mm.h, mmv)
                if tg >= 1:
                    t.wait_ge(s_xf.h, xfv)
                for ci in range(NCI):
                    ins = t.transpose(bankb(4 + ci // 8)[:, (ci % 8) * 128:(ci % 8 + 1) * 128],
                                      in_=v_xntok[:, ci * 128:(ci + 1) * 128],
                                      identity=ident[:, :])
                ins.then_inc(s_tp.h, 1)
            s_tp.inc()
            @blk.scalar
            def _(c, tv=s_tp.n, tt4=tt4):
                c.wait_ge(s_tp.h, tv)
                for half in range(2):
                    src = bankb(4 + half).rearrange("p (s q) -> p s q", q=128)
                    dst = v_xnfm.rearrange("p (ci t) -> p ci t", t=TC)[
                        :, 8 * half:8 * half + 8, tt4 * 128:(tt4 + 1) * 128]
                    c.copy(out=dst, in_=src).then_inc(s_xf.h, 1)
            s_xf.inc(2)
        XF_CHUNK = s_xf.n

        for f8 in range(8):
            hh_ = f8 // 2
            qk = f8 % 2
            bk = f8 % 2
            @blk.tensor
            def _(t, hh_=hh_, qk=qk, bk=bk, f8=f8, xf=XF_CHUNK, qb=QE_BASE):
                t.wait_ge(s_xf.h, xf)
                t.wait_ge(s_qe.h, qb + max(0, f8 - 1))
                fo = (qk * HPC + hh_) * 128
                for ci in range(NCI):
                    ins = t.matmul(bank(bk), lhsT=v_wqkvT[:, ci * 1536 + fo:ci * 1536 + fo + 128],
                                   rhs=v_xnfm[:, ci * TC:(ci + 1) * TC],
                                   start=(ci == 0), stop=(ci == NCI - 1))
                ins.then_inc(s_mm.h, 1)
            s_mm.inc()
            @blk.vector
            def _(v, mv=s_mm.n, hh_=hh_, qk=qk, bk=bk, tcn=tcn):
                v.wait_ge(s_mm.h, mv)
                dst = (qst if qk == 0 else kst)[:, hh_ * S + tcn * TC: hh_ * S + (tcn + 1) * TC]
                v.tensor_scalar(out=dst, in0=bank(bk), scalar1=(SCALE if qk == 0 else 1.0),
                                scalar2=None, op0=ALU.mult).then_inc(s_qe.h, 1)
            s_qe.inc()
        for tt4 in range(4):
            bk = 2 + tt4 % 2
            @blk.tensor
            def _(t, tt4=tt4, bk=bk, xf=XF_CHUNK, qb=QE_BASE, tcn=tcn):
                t.wait_ge(s_xf.h, xf)
                if tt4 >= 2:
                    t.wait_ge(s_qe.h, qb + 9 + (tt4 - 2))
                elif tcn >= 1:
                    t.wait_ge(s_qe.h, qb - 1 + tt4)
                for ci in range(NCI):
                    ins = t.matmul(bank(bk), lhsT=v_xnfm[:, ci * TC + tt4 * 128:ci * TC + (tt4 + 1) * 128],
                                   rhs=v_wqkvT[:, ci * 1536 + 1024:ci * 1536 + 1536],
                                   start=(ci == 0), stop=(ci == NCI - 1))
                ins.then_inc(s_mm.h, 1)
            s_mm.inc()
            @blk.vector
            def _(v, mv=s_mm.n, tt4=tt4, bk=bk, tcn=tcn):
                v.wait_ge(s_mm.h, mv)
                tg = tcn * 4 + tt4
                v.tensor_scalar(out=vst[:, tg * 512:(tg + 1) * 512], in0=bank(bk),
                                scalar1=1.0, scalar2=None, op0=ALU.mult).then_inc(s_qe.h, 1)
            s_qe.inc()
    P1_MM = s_mm.n
    P1_QE = s_qe.n

    P1_LN = s_ln.n
    P1_SQ = s_sq.n
    P1_TP = s_tp.n
    XF_ALL = s_xf.n
    CB = 0

    GBLK = [(m, b2) for m in range(2) for b2 in range(GUB)] + \
           [(2, b2) for b2 in range(NCI)]
    n_gblk = len(GBLK)

    slot_guard = [(s_xf, XF_ALL), (s_wte, s_wte.n)]
    slot_state = [0]

    def slotb(i):
        return bankb(4 + 2 * i)

    def slotf(i):
        return ps[:, (4 + 2 * i) * 512:(4 + 2 * i) * 512 + 512]

    def alloc_slot(pair=False):
        i = slot_state[0] % 2
        slot_state[0] += 1
        return i, [slot_guard[i]]

    def emit_gather_block(gb_i):
        m, bi2 = GBLK[gb_i]
        idx_d = (d_gidx, d_uidx, d_didx)[m]
        if m < 2:
            n_ot = min(4, OT - bi2 * 4)
            n = n_ot * CSH
            nblk = n_ot * 2
        else:
            n_ot = 1
            n = DSH
            nblk = DOT
        gst = (g_stA, g_stB)[gb_i % 2]
        @blk.sync
        def _(sy, m=m, bi2=bi2, n_ot=n_ot, n=n, idx_d=idx_d, sel=s_gsel.n,
              gb_i=gb_i, lnp=P1_LN, sqp=P1_SQ):
            if gb_i == 0:
                sy.wait_ge(s_ln.h, lnp)
                sy.wait_ge(s_sq.h, sqp)
            else:
                sy.wait_ge(s_gsel.h, sel)
            if m < 2:
                src = idx_d[bi2 * 512:bi2 * 512 + n_ot * 128, :].rearrange(
                    "(o p) c -> p o c", p=128)
                dst = g_idx[:, 0:n].rearrange("p (o c) -> p o c", o=n_ot)
            else:
                src = idx_d[bi2 * 128:(bi2 + 1) * 128, :]
                dst = g_idx[:, 0:n]
            sy.dma_start(out=dst, in_=src).then_inc(s_idx.h, 16)
        s_idx.inc(16)
        @blk.vector
        def _(v, n=n, iv=s_idx.n, gtp=s_gtp.n, gb_i=gb_i, mmp=P1_MM,
              tpp=P1_TP):
            v.wait_ge(s_idx.h, iv)
            if gb_i == 0:
                v.wait_ge(s_mm.h, mmp)
                v.wait_ge(s_tp.h, tpp)
            else:
                v.wait_ge(s_gtp.h, gtp)
            ins = v.tensor_copy(out=g_gg[:, 0:n], in_=g_idx[:, 0:n])
            hop(v, s_vh, ins)
            v.tensor_scalar(out=g_w[:, 0:n], in0=g_gg[:, 0:n], scalar1=0,
                            scalar2=lutc[:, 0:1], op0=ALU.is_equal, op1=ALU.mult)
            for j in range(1, 16):
                ins = v.tensor_scalar(out=g_g2s[:, 0:n], in0=g_gg[:, 0:n],
                                      scalar1=j, scalar2=lutc[:, j:j + 1],
                                      op0=ALU.is_equal, op1=ALU.mult)
                hop(v, s_vh, ins)
                ins = v.tensor_tensor(out=g_w[:, 0:n], in0=g_w[:, 0:n],
                                      in1=g_g2s[:, 0:n], op=ALU.add)
                if j < 15:
                    hop(v, s_vh, ins)
            ins.then_inc(s_gsel.h, 1)
        s_gsel.inc()
        sl, guards = alloc_slot()
        @blk.tensor
        def _(t, m=m, n_ot=n_ot, nblk=nblk, sel=s_gsel.n, sl=sl,
              guards=tuple(guards)):
            t.wait_ge(s_gsel.h, sel)
            for sem, cnt in guards:
                t.wait_ge(sem.h, cnt)
            bb = slotb(sl)
            ins = None
            for ci in range(nblk):
                if m < 2:
                    o_, ct_ = ci % n_ot, ci // n_ot
                    src = g_w[:, o_ * CSH + ct_ * 128:o_ * CSH + ct_ * 128 + 128]
                else:
                    src = g_w[:, ci * 128:(ci + 1) * 128]
                ins = t.transpose(bb[:, ci * 128:(ci + 1) * 128], in_=src,
                                  identity=ident[:, :])
            ins.then_inc(s_gtp.h, 1)
        s_gtp.inc()
        @blk.scalar
        def _(c, nblk=nblk, tv=s_gtp.n, sl=sl, gsd=s_gsd.n, gb_i=gb_i,
              gst=gst):
            c.wait_ge(s_gtp.h, tv)
            if gb_i >= 2:
                c.wait_ge(s_gsd.h, gsd)
            bb = slotb(sl)
            c.copy(out=gst[:, 0:nblk * 128],
                   in_=bb[:, 0:nblk * 128]).then_inc(s_gte.h, 1)
        s_gte.inc()
        slot_guard[sl] = (s_gte, s_gte.n)
        @blk.sync
        def _(sy, m=m, bi2=bi2, n_ot=n_ot, nblk=nblk, ge=s_gte.n, gst=gst):
            sy.wait_ge(s_gte.h, ge)
            if m < 2:
                dd = (d_agg_i, d_agu_i)[m]
                for ct_ in range(2):
                    dst = dd.ap().rearrange("(ct o) p q -> p ct o q", o=OT)[
                        :, ct_, bi2 * 4:bi2 * 4 + n_ot]
                    src = gst[:, ct_ * n_ot * 128:(ct_ + 1) * n_ot * 128].rearrange(
                        "p (o q) -> p o q", q=128)
                    sy.dma_start(out=dst, in_=src).then_inc(s_gsd.h, 16)
            else:
                dst = d_agd_i.ap().rearrange("(o e) p q -> p o e q", e=NCI)[
                    :, :, bi2]
                src = gst[:, 0:nblk * 128].rearrange("p (o q) -> p o q", q=128)
                sy.dma_start(out=dst, in_=src).then_inc(s_gsd.h, 16)
        s_gsd.inc(32 if m < 2 else 16)
        last = (m < 2 and bi2 == GUB - 1) or (m == 2 and bi2 == NCI - 1)
        if last:
            ccs = (s_ccg, s_ccu, s_ccd)[m]
            @blk.gpsimd
            def _(g, m=m, sd=s_gsd.n, ccs=ccs):
                g.wait_ge(s_gsd.h, sd)
                pr = [(d_agg_i, d_agg_o), (d_agu_i, d_agu_o),
                      (d_agd_i, d_agd_o)][m]
                g.collective_compute(
                    "AllGather", ALU.bypass,
                    replica_groups=[list(range(NCORE))],
                    ins=[pr[0].ap().opt()], outs=[pr[1].ap().opt()],
                ).then_inc(ccs.h, 1)
            ccs.inc()

    st2 = {"blk": 0, "first": True, "gb": 0}

    def emit_scores(hh_, qt):
        nk = qt + 1
        nb = (nk + 3) // 4
        EXPP = s_exp.n
        for kb in range(nb):
            w = min(512, nk * 128 - kb * 512)
            diag = (kb == nb - 1)
            bi = st2["blk"]
            @blk.tensor
            def _(t, hh_=hh_, qt=qt, kb=kb, w=w, bi=bi, qe=P1_QE):
                if bi < 2:
                    t.wait_ge(s_qe.h, qe)
                else:
                    t.wait_ge(s_se.h, 2 * bi - 2)
                ins = None
                for half in range(2):
                    ins = t.matmul(ps[:, (2 * (bi % 2) + half) * 512:
                                      (2 * (bi % 2) + half) * 512 + w],
                                   lhsT=qst[64 * half:64 * half + 64,
                                            hh_ * S + qt * 128:hh_ * S + (qt + 1) * 128],
                                   rhs=kst[64 * half:64 * half + 64,
                                           hh_ * S + kb * 512:hh_ * S + kb * 512 + w],
                                   start=True, stop=True,
                                   tile_position=(64 * half, 0))
                ins.then_inc(s_sc.h, 1)
            s_sc.inc()
            if diag:
                @blk.vector
                def _(v, kb=kb, w=w, sc=s_sc.n, bi=bi):
                    v.wait_ge(s_sc.h, sc)
                    if bi >= 2:
                        v.wait_ge(s_se.h, 2 * bi - 2)
                    ins = None
                    for half in range(2):
                        bk_ = ps[:, (2 * (bi % 2) + half) * 512 + w - 128:
                                 (2 * (bi % 2) + half) * 512 + w]
                        ins = v.tensor_tensor(out=bk_, in0=bk_, in1=dmn[:, :],
                                              op=ALU.add)
                    ins.then_inc(s_msk.h, 1)
                s_msk.inc()
            @blk.vector
            def _(v, kb=kb, w=w, sc=s_sc.n, mk=s_msk.n, ep=EXPP, bi=bi,
                  mxp=s_mx.n, first=st2["first"], mmp=P1_MM):
                v.wait_ge(s_sc.h, sc)
                v.wait_ge(s_msk.h, mk)
                v.wait_ge(s_exp.h, ep)
                if kb == 0:
                    v.wait_ge(s_mx.h, mxp)
                if first:
                    v.wait_ge(s_mm.h, mmp)
                v.tensor_scalar(out=s1b[:, kb * 512:kb * 512 + w],
                                in0=ps[:, (2 * (bi % 2)) * 512:
                                       (2 * (bi % 2)) * 512 + w],
                                scalar1=1.0, scalar2=None, op0=ALU.mult,
                                op1=ALU.max,
                                accum_out=smtmp[:, 16 + kb:17 + kb]).then_inc(s_se.h, 1)
                v.tensor_scalar(out=s2b[:, kb * 512:kb * 512 + w],
                                in0=ps[:, (2 * (bi % 2) + 1) * 512:
                                       (2 * (bi % 2) + 1) * 512 + w],
                                scalar1=1.0, scalar2=None, op0=ALU.mult,
                                op1=ALU.max,
                                accum_out=smtmp[:, 24 + kb:25 + kb]).then_inc(s_se.h, 1)
            s_se.inc(2)
            st2["first"] = False
            st2["blk"] += 1
        @blk.vector
        def _(v, nb=nb, sev=s_se.n, expp=EXPP, lnp=P1_LN):
            v.wait_ge(s_se.h, sev)
            v.wait_ge(s_exp.h, expp)
            v.wait_ge(s_ln.h, lnp)
            v.tensor_reduce(out=csc[:, 11:12], in_=smtmp[:, 16:16 + nb],
                            axis=AXL.X, op=ALU.max, negate=True)
            v.tensor_reduce(out=csc[:, 12:13], in_=smtmp[:, 24:24 + nb],
                            axis=AXL.X, op=ALU.max, negate=True).then_inc(s_mx.h, 1)
        s_mx.inc()
        return s_mx.n

    def emit_exp(hh_, qt, mx_n):
        nk = qt + 1
        @blk.scalar
        def _(c, nk=nk, mx=mx_n, ptpp=s_ptp.n, rrp=s_rr.n, mmp=P1_MM,
              expn=s_exp.n):
            c.wait_ge(s_mx.h, mx)
            c.wait_ge(s_ptp.h, ptpp)
            c.wait_ge(s_rr.h, rrp)
            if expn == 0:
                c.wait_ge(s_mm.h, mmp)
            c.activation(out=p1b[:, 0:nk * 128], in_=s1b[:, 0:nk * 128],
                         func=AF.Exp, bias=csc[:, 11:12],
                         accum_out=csc[:, 13:14]).then_inc(s_exp.h, 1)
            c.activation(out=p2b[:, 0:nk * 128], in_=s2b[:, 0:nk * 128],
                         func=AF.Exp, bias=csc[:, 12:13],
                         accum_out=csc[:, 14:15]).then_inc(s_exp.h, 1)
        s_exp.inc(2)

    def emit_post(hh_, qt):
        nk = qt + 1
        qtc = qt % 4
        qc = qt // 4
        @blk.vector
        def _(v, ex=s_exp.n, dtp=s_dtp.n, lnp=P1_LN, rrn=s_rr.n):
            v.wait_ge(s_exp.h, ex)
            v.wait_ge(s_dtp.h, dtp)
            if rrn == 0:
                v.wait_ge(s_ln.h, lnp)
            v.reciprocal(out=csc[:, 15:16], in_=csc[:, 13:14])
            v.reciprocal(out=csc[:, 10:11], in_=csc[:, 14:15]).then_inc(s_rr.h, 1)
        s_rr.inc()
        EXP_QT = s_exp.n
        AVD_PREV = s_avd.n
        for half, pb, ptb in ((0, p1b, pt1), (1, p2b, pt2)):
            first_evac = True
            for b0 in range(0, nk, 8):
                jn = min(8, nk - b0)
                sl, guards = alloc_slot()
                @blk.tensor
                def _(t, b0=b0, jn=jn, sl=sl, guards=tuple(guards), ex=EXP_QT,
                      pb=pb):
                    t.wait_ge(s_exp.h, ex)
                    for sem, cnt in guards:
                        t.wait_ge(sem.h, cnt)
                    ins = None
                    for j in range(jn):
                        ins = t.transpose(slotb(sl)[:, j * 128:(j + 1) * 128],
                                          in_=pb[:, (b0 + j) * 128:(b0 + j + 1) * 128],
                                          identity=ident[:, :])
                    ins.then_inc(s_ptp.h, 1)
                s_ptp.inc()
                @blk.scalar
                def _(c, b0=b0, jn=jn, sl=sl, tv=s_ptp.n, avdp=AVD_PREV,
                      first=first_evac, ptb=ptb, mmp=P1_MM, pten=s_pte.n):
                    c.wait_ge(s_ptp.h, tv)
                    if first:
                        c.wait_ge(s_avd.h, avdp)
                    if pten == 0:
                        c.wait_ge(s_mm.h, mmp)
                    c.copy(out=ptb[:, b0 * 128:(b0 + jn) * 128],
                           in_=slotb(sl)[:, 0:jn * 128]).then_inc(s_pte.h, 1)
                s_pte.inc()
                slot_guard[sl] = (s_pte, s_pte.n)
                first_evac = False
        sl, guards = alloc_slot()
        @blk.tensor
        def _(t, rrv=s_rr.n, sl=sl, guards=tuple(guards)):
            t.wait_ge(s_rr.h, rrv)
            for sem, cnt in guards:
                t.wait_ge(sem.h, cnt)
            t.transpose(slotf(sl)[0:1, 0:128], in_=csc[:, 15:16],
                        identity=identf[:, :])
            t.transpose(slotf(sl)[0:1, 128:256], in_=csc[:, 10:11],
                        identity=identf[:, :]).then_inc(s_dtp.h, 1)
        s_dtp.inc()
        @blk.scalar
        def _(c, dtp=s_dtp.n, bc=s_bc.n, sl=sl):
            c.wait_ge(s_dtp.h, dtp)
            c.wait_ge(s_bc.h, bc)
            c.copy(out=rr, in_=slotf(sl)[0:1, 0:256]).then_inc(s_rre.h, 1)
        s_rre.inc()
        slot_guard[sl] = (s_rre, s_rre.n)
        @blk.gpsimd
        def _(g, rre=s_rre.n, qtc=qtc, cmbp=s_cmb.n):
            g.wait_ge(s_rre.h, rre)
            if qtc == 0:
                g.wait_ge(s_cmb.h, cmbp)
            g.partition_broadcast(R1bc[:, qtc * 128:(qtc + 1) * 128],
                                  rr[0:1, 0:128])
            g.partition_broadcast(R2bc[:, qtc * 128:(qtc + 1) * 128],
                                  rr[0:1, 128:256]).then_inc(s_bc.h, 1)
        s_bc.inc()
        @blk.tensor
        def _(t, hh_=hh_, nk=nk, qtc=qtc, pte=s_pte.n, cmbp=s_cmb.n):
            t.wait_ge(s_pte.h, pte)
            if qtc == 0:
                t.wait_ge(s_cmb.h, cmbp)
            ins = None
            for kt in range(nk):
                vt = vst[:, kt * 512 + hh_ * 128:kt * 512 + (hh_ + 1) * 128]
                st_ = (qtc == 0 and kt == 0)
                sp_ = (qtc == 3 and kt == nk - 1)
                t.matmul(ps[:, 5 * 512 + qtc * 128:5 * 512 + (qtc + 1) * 128],
                         lhsT=vt, rhs=pt1[:, kt * 128:(kt + 1) * 128],
                         start=st_, stop=sp_)
                ins = t.matmul(ps[:, 7 * 512 + qtc * 128:7 * 512 + (qtc + 1) * 128],
                               lhsT=vt, rhs=pt2[:, kt * 128:(kt + 1) * 128],
                               start=st_, stop=sp_)
            ins.then_inc(s_avd.h, 1)
        s_avd.inc()
        if qtc != 3:
            return
        @blk.vector
        def _(v, av=s_avd.n, bc=s_bc.n, a2=s_a2.n, gnp=s_gn.n, lnp=P1_LN,
              sqp=P1_SQ, xfp=XF_ALL, cmbn=s_cmb.n):
            v.wait_ge(s_avd.h, av)
            v.wait_ge(s_bc.h, bc)
            v.wait_ge(s_a2.h, a2)
            v.wait_ge(s_gn.h, gnp)
            if cmbn == 0:
                v.wait_ge(s_ln.h, lnp)
                v.wait_ge(s_sq.h, sqp)
                v.wait_ge(s_xf.h, xfp)
            ins = v.tensor_tensor(out=sct512A, in0=ps[:, 5 * 512:6 * 512],
                                  in1=R1bc, op=ALU.mult)
            hop(v, s_vh, ins)
            ins = v.tensor_tensor(out=sct512B, in0=ps[:, 7 * 512:8 * 512],
                                  in1=R2bc, op=ALU.mult)
            hop(v, s_vh, ins)
            v.scalar_tensor_tensor(out=v_A, in0=sct512B, scalar=lamneg[:, :],
                                   in1=sct512A, op0=ALU.mult,
                                   op1=ALU.add).then_inc(s_cmb.h, 1)
        s_cmb.inc()
        @blk.gpsimd
        def _(g, cm=s_cmb.n, stv=s_st.n):
            g.wait_ge(s_cmb.h, cm)
            g.wait_ge(s_st.h, stv)
            g.tensor_tensor(out=v_A2, in0=v_A, in1=v_A, op=ALU.mult).then_inc(s_a2.h, 1)
        s_a2.inc()
        sg = tuple(set(slot_guard))
        @blk.tensor
        def _(t, a2=s_a2.n, sg=sg):
            t.wait_ge(s_a2.h, a2)
            for sem, cnt in sg:
                t.wait_ge(sem.h, cnt)
            t.matmul(drow(4), lhsT=ones1[:, :], rhs=v_A, start=True, stop=True)
            t.matmul(drow(6), lhsT=ones1[:, :], rhs=v_A2, start=True,
                     stop=True).then_inc(s_st.h, 1)
        s_st.inc()
        @blk.scalar
        def _(c, st=s_st.n, rm=s_rm.n):
            c.wait_ge(s_st.h, st)
            c.wait_ge(s_rm.h, rm)
            c.copy(out=rbc[0:1, 2 * 512:3 * 512], in_=drow(4))
            c.copy(out=rbc[0:1, 3 * 512:4 * 512], in_=drow(6)).then_inc(s_ste.h, 1)
        s_ste.inc()
        for i in range(2):
            slot_guard[i] = (s_ste, s_ste.n)
        slot_state[0] = 0
        @blk.gpsimd
        def _(g, ste=s_ste.n, gn=s_gn.n):
            g.wait_ge(s_ste.h, ste)
            if s_rm.n >= 1:
                g.wait_ge(s_gn.h, gn)
            g.tensor_scalar(out=rbc[0:1, 4 * 512:5 * 512], in0=rbc[0:1, 2 * 512:3 * 512],
                            scalar1=1.0 / HD, scalar2=None, op0=ALU.mult)
            ins = g.tensor_scalar(out=rbc[0:1, 5 * 512:6 * 512], in0=rbc[0:1, 3 * 512:4 * 512],
                                  scalar1=1.0 / HD, scalar2=None, op0=ALU.mult)
            hop(g, s_gh, ins)
            ins = g.tensor_tensor(out=rbc[0:1, 6 * 512:7 * 512], in0=rbc[0:1, 4 * 512:5 * 512],
                                  in1=rbc[0:1, 4 * 512:5 * 512], op=ALU.mult)
            hop(g, s_gh, ins)
            g.tensor_tensor(out=rbc[0:1, 6 * 512:7 * 512], in0=rbc[0:1, 5 * 512:6 * 512],
                            in1=rbc[0:1, 6 * 512:7 * 512], op=ALU.subtract).then_inc(s_rm.h, 1)
        s_rm.inc()
        @blk.scalar
        def _(c, rm=s_rm.n):
            c.wait_ge(s_rm.h, rm)
            ins = c.activation(out=rbc[0:1, 7 * 512:8 * 512], in_=rbc[0:1, 6 * 512:7 * 512],
                               func=AF.Ln, bias=epsb[0:1, 0:1])
            hop(c, s_ch2, ins)
            c.activation(out=rbc[0:1, 2 * 512:3 * 512], in_=rbc[0:1, 7 * 512:8 * 512],
                         func=AF.Exp, scale=-0.5).then_inc(s_rv.h, 1)
        s_rv.inc()
        @blk.gpsimd
        def _(g, rv=s_rv.n, gnp=s_gn.n):
            g.wait_ge(s_rv.h, rv)
            g.wait_ge(s_gn.h, gnp)
            ins = g.tensor_tensor(out=rbc[0:1, 3 * 512:4 * 512], in0=rbc[0:1, 4 * 512:5 * 512],
                                  in1=rbc[0:1, 2 * 512:3 * 512], op=ALU.mult)
            hop(g, s_gh, ins)
            g.partition_broadcast(rbc[:, 2 * 512:4 * 512], rbc[0:1, 2 * 512:4 * 512]).then_inc(s_rm.h, 1)
        s_rm.inc()
        @blk.vector
        def _(v, rm=s_rm.n, hh_=hh_, gnd=s_gnd.n):
            v.wait_ge(s_rm.h, rm)
            if s_gn.n >= 1:
                v.wait_ge(s_gnd.h, gnd)
            ins = v.tensor_tensor(out=sct512A, in0=v_A, in1=rbc[:, 2 * 512:3 * 512], op=ALU.mult)
            hop(v, s_vh, ins)
            ins = v.tensor_tensor(out=sct512B, in0=sct512A,
                                  in1=rbc[:, 3 * 512:4 * 512], op=ALU.subtract)
            hop(v, s_vh, ins)
            v.tensor_scalar(out=v_gn, in0=sct512B, scalar1=gbv[:, hh_:hh_ + 1],
                            scalar2=gbv[:, 4 + hh_:5 + hh_], op0=ALU.mult,
                            op1=ALU.add).then_inc(s_gn.h, 1)
        s_gn.inc()
        @blk.sync
        def _(sy, gn=s_gn.n, hh_=hh_, qc=qc):
            sy.wait_ge(s_gn.h, gn)
            sy.dma_start(out=d_agn_i[hh_ * 128:(hh_ + 1) * 128, qc * TC:(qc + 1) * TC],
                         in_=v_gn).then_inc(s_gnd.h, 16)
        s_gnd.inc(16)
        if K_NOGATHER:
            return
        it = hh_ * QC + qc
        want = n_gblk if K_GATHER_FIRST else (it + 1) * n_gblk // (HPC * QC)
        while st2["gb"] < min(want, n_gblk):
            emit_gather_block(st2["gb"])
            st2["gb"] += 1

    pend = None
    for hh_ in range(HPC):
        for qt in range(NTT):
            if hh_ * NTT + qt >= K_QTLIM:
                break
            mx_n = emit_scores(hh_, qt)
            if pend is not None:
                emit_post(*pend)
            emit_exp(hh_, qt, mx_n)
            pend = (hh_, qt)
    if pend is not None:
        emit_post(*pend)
    while (not K_NOGATHER) and st2["gb"] < n_gblk:
        emit_gather_block(st2["gb"])
        st2["gb"] += 1
    GND_ALL = s_gnd.n

    @blk.gpsimd
    def _(g, gnd=GND_ALL):
        g.wait_ge(s_gnd.h, gnd)
        g.collective_compute(
            "AllGather", ALU.bypass,
            replica_groups=[[0, 1, 2, 3], [4, 5, 6, 7]],
            ins=[d_agn_i.ap().opt()], outs=[d_agn_o.ap().opt()],
        ).then_inc(s_ccn.h, 1)
    s_ccn.inc()

    if "gn" in taps:
        @blk.sync
        def _(sy, gnd=GND_ALL):
            sy.wait_ge(s_gnd.h, gnd)
            sy.dma_start(out=tap_dram["gn"], in_=d_agn_i.ap()).then_inc(s_out.h, 16)
        s_out.inc(16)
    if "wg" in taps:
        @blk.sync
        def _(sy):
            sy.wait_ge(s_ccg.h, 1)
            sy.dma_start(out=tap_dram["wg"], in_=d_agg_o.ap()).then_inc(s_out.h, 16)
        s_out.inc(16)

    if phases < 3:
        @blk.sync
        def _(sy, n=s_out.n, gnd=GND_ALL, cmbn=s_cmb.n):
            if not K_NOGATHER:
                sy.wait_ge(s_ccg.h, 1)
                sy.wait_ge(s_ccu.h, 1)
                sy.wait_ge(s_ccd.h, 1)
            sy.wait_ge(s_ccn.h, 1)
            sy.wait_ge(s_gnd.h, gnd)
            sy.wait_ge(s_cmb.h, cmbn)
            sy.wait_ge(s_out.h, n)
        _replay()
        es.close()
        return nc, tap_dram

    GSEL_ALL = s_gsel.n
    GTE_ALL = s_gte.n
    GN_ALL = s_gn.n
    RM_ALL = s_rm.n
    for tg in range(TOWN // 128):
        @blk.sync
        def _(sy, tg=tg, lnv=s_ln.n, sqv=s_sq.n, ga=GSEL_ALL):
            if tg == 0:
                sy.wait_ge(s_gsel.h, ga)
            else:
                sy.wait_ge(s_ln.h, lnv)
                sy.wait_ge(s_sq.h, sqv)
            sy.dma_start(out=v_x[:, :], in_=d_xo[tg * 128:(tg + 1) * 128, :]).then_inc(s_xo.h, 16)
        s_xo.inc(16)
        xv = s_xo.n
        @blk.scalar
        def _(c, xv=xv, tg=tg, gna=GN_ALL):
            c.wait_ge(s_xo.h, xv)
            if tg == 0:
                c.wait_ge(s_gn.h, gna)
            c.activation(out=v_scr2, in_=v_x[:, :], func=AF.Square,
                         accum_out=csc[:, 1:2]).then_inc(s_sq.h, 1)
        s_sq.inc()
        @blk.vector
        def _(v, xv=xv, sq=s_sq.n, tg=tg, gna=GN_ALL, rma=RM_ALL):
            v.wait_ge(s_xo.h, xv)
            if tg == 0:
                v.wait_ge(s_gn.h, gna)
                v.wait_ge(s_rm.h, rma)
            ins = v.tensor_scalar(out=v_scr1, in0=v_x[:, :], scalar1=1.0, scalar2=None,
                                  op0=ALU.mult, op1=ALU.add, accum_out=csc[:, 0:1])
            hop(v, s_vh, ins)
            v.wait_ge(s_sq.h, sq)
            v.tensor_scalar(out=csc[:, 2:3], in0=csc[:, 0:1], scalar1=1.0 / D,
                            scalar2=None, op0=ALU.mult)
            ins = v.tensor_scalar(out=csc[:, 3:4], in0=csc[:, 1:2], scalar1=1.0 / D,
                                  scalar2=None, op0=ALU.mult)
            hop(v, s_vh, ins)
            ins = v.tensor_tensor(out=csc[:, 4:5], in0=csc[:, 2:3], in1=csc[:, 2:3], op=ALU.mult)
            hop(v, s_vh, ins)
            v.tensor_tensor(out=csc[:, 6:7], in0=csc[:, 3:4], in1=csc[:, 4:5],
                            op=ALU.subtract).then_inc(s_rv.h, 1)
        s_rv.inc()
        @blk.scalar
        def _(c, rv=s_rv.n):
            c.wait_ge(s_rv.h, rv)
            ins = c.activation(out=csc[:, 7:8], in_=csc[:, 6:7], func=AF.Ln,
                               bias=epsb[:, 0:1])
            hop(c, s_ch2, ins)
            c.activation(out=csc[:, 8:9], in_=csc[:, 7:8], func=AF.Exp,
                         scale=-0.5).then_inc(s_rv.h, 1)
        s_rv.inc()
        @blk.vector
        def _(v, rv=s_rv.n, tg=tg, tpv=s_tp.n):
            v.wait_ge(s_rv.h, rv)
            ins = v.scalar_tensor_tensor(out=csc[:, 10:11], in0=csc[:, 8:9],
                                         scalar=-1.0, in1=csc[:, 2:3],
                                         op0=ALU.mult, op1=ALU.mult)
            hop(v, s_vh, ins)
            if tg >= 1:
                v.wait_ge(s_tp.h, tpv)
            v.tensor_scalar(out=v_xntok, in0=v_x[:, :], scalar1=csc[:, 8:9],
                            scalar2=csc[:, 10:11], op0=ALU.mult, op1=ALU.add).then_inc(s_ln.h, 1)
        s_ln.inc()
        @blk.tensor
        def _(t, lnv=s_ln.n, tg=tg, xfv=s_xf.n, avd=s_avd.n, cmbv=s_cmb.n,
              rrv=s_rr.n, stev=s_ste.n, gtev=s_gte.n, stv=s_st.n, rmv=s_rm.n):
            t.wait_ge(s_ln.h, lnv)
            if tg == 0:
                t.wait_ge(s_avd.h, avd)
                t.wait_ge(s_cmb.h, cmbv)
                t.wait_ge(s_rr.h, rrv)
                t.wait_ge(s_ste.h, stev)
                t.wait_ge(s_st.h, stv)
                t.wait_ge(s_rm.h, rmv)
                t.wait_ge(s_gte.h, gtev)
            if tg >= 1:
                t.wait_ge(s_xf.h, xfv)
            for ci in range(NCI):
                ins = t.transpose(bankb(4 + ci // 8)[:, (ci % 8) * 128:(ci % 8 + 1) * 128],
                                  in_=v_xntok[:, ci * 128:(ci + 1) * 128],
                                  identity=ident[:, :])
            ins.then_inc(s_tp.h, 1)
        s_tp.inc()
        @blk.scalar
        def _(c, tv=s_tp.n, tg=tg):
            c.wait_ge(s_tp.h, tv)
            for half in range(2):
                src = bankb(4 + half).rearrange("p (s q) -> p s q", q=128)
                dst = v_xnown.rearrange("p (ci t) -> p ci t", t=TOWN)[
                    :, 8 * half:8 * half + 8, tg * 128:(tg + 1) * 128]
                c.copy(out=dst, in_=src).then_inc(s_xf.h, 1)
        s_xf.inc(2)
    XNOWN = s_xf.n

    NTCH = TOWN // TC
    assert TOWN % TC == 0 or TOWN < TC
    tchs = list(range(max(1, TOWN // TC)))
    TW = min(TC, TOWN)
    for ot in range(OT):
        bi = ot % 2
        sw = s_wgP[bi]
        @blk.sync
        def _(sy, ot=ot, bi=bi, fm=s_fmm.n, gnd=GND_ALL, sw=sw, avdf=s_avd.n):
            sy.wait_ge(s_ccg.h, 1)
            sy.wait_ge(s_ccu.h, 1)
            sy.wait_ge(s_ccd.h, 1)
            if ot == 0:
                sy.wait_ge(s_gnd.h, gnd)
                sy.wait_ge(s_avd.h, avdf)
            if ot >= 2:
                sy.wait_ge(s_fmm.h, fm)
            for (dd, vw) in ((d_agg_o, v_wg[bi]), (d_agu_o, v_wu[bi])):
                src = dd.ap().rearrange("(ci ot) p q -> ci ot p q", ot=OT)[:, ot].rearrange(
                    "ci p q -> p ci q")
                sy.dma_start(out=vw.rearrange("p (ci q) -> p ci q", q=128),
                             in_=src).then_inc(sw.h, 16)
        sw.inc(32)
        wv = sw.n
        @blk.tensor
        def _(t, ot=ot, bi=bi, wv=wv, xn=XNOWN, hv=s_h.n, tchs=tchs, TW=TW, sw=sw):
            t.wait_ge(sw.h, wv)
            t.wait_ge(s_xf.h, xn)
            if ot >= 2:
                t.wait_ge(s_h.h, hv)
            grp = (ot % 2) * 4
            for ci in range(NCI):
                st_ = (ci == 0)
                sp_ = (ci == NCI - 1)
                for tch in tchs:
                    t.matmul(bank(grp + tch, TW), lhsT=v_wg[bi][:, ci * 128:(ci + 1) * 128],
                             rhs=v_xnown[:, ci * TOWN + tch * TW:ci * TOWN + (tch + 1) * TW],
                             start=st_, stop=sp_)
                    ins = t.matmul(bank(grp + 2 + tch, TW), lhsT=v_wu[bi][:, ci * 128:(ci + 1) * 128],
                                   rhs=v_xnown[:, ci * TOWN + tch * TW:ci * TOWN + (tch + 1) * TW],
                                   start=st_, stop=sp_)
            ins.then_inc(s_fmm.h, 1)
        s_fmm.inc()
        @blk.scalar
        def _(c, ot=ot, fm=s_fmm.n, hv=s_h.n, tchs=tchs, TW=TW):
            c.wait_ge(s_fmm.h, fm)
            if ot >= 1:
                c.wait_ge(s_h.h, hv)
            grp = (ot % 2) * 4
            for tch in tchs:
                ins = c.activation(out=(v_gt0 if tch == 0 else v_gt1)[:, 0:TW],
                                   in_=bank(grp + tch, TW), func=AF.Silu)
            ins.then_inc(s_si.h, 1)
        s_si.inc()
        @blk.vector
        def _(v, ot=ot, si=s_si.n, tchs=tchs, TW=TW):
            v.wait_ge(s_si.h, si)
            grp = (ot % 2) * 4
            for tch in tchs:
                ins = v.tensor_tensor(out=h_sl(ot, tch * TW, TW), in0=bank(grp + 2 + tch, TW),
                                      in1=(v_gt0 if tch == 0 else v_gt1)[:, 0:TW], op=ALU.mult)
            ins.then_inc(s_h.h, 1)
        s_h.inc()
    H_DONE = s_h.n

    if "h" in taps:
        @blk.sync
        def _(sy, hv=H_DONE):
            sy.wait_ge(s_h.h, hv)
            for ot in range(OT):
                sy.dma_start(out=tap_dram["h"][:, ot * TOWN:(ot + 1) * TOWN],
                             in_=h_sl(ot, 0, TOWN)).then_inc(s_out.h, 16)
        s_out.inc(16 * OT)

    if phases < 4:
        @blk.sync
        def _(sy, n=s_out.n, hv=H_DONE):
            sy.wait_ge(s_h.h, hv)
            sy.wait_ge(s_ccn.h, 1)
            sy.wait_ge(s_out.h, n)
        _replay()
        es.close()
        return nc, tap_dram

    @blk.gpsimd
    def _(g, hv=H_DONE, avd=s_avd.n):
        g.wait_ge(s_ccn.h, 1)
        g.wait_ge(s_avd.h, avd)
        g.wait_ge(s_h.h, hv)
        with g.register("rk") as rk:
            g.cc_rank_ld(rk, replica_groups=[[0, 1, 2, 3], [4, 5, 6, 7]])
            g.reg_mul(rk, rk, TOWN)
            off = g.snap(rk)
            src = d_agn_o.rearrange("(fc p) t -> p fc t", p=128)[:, :, bass.ds(off, TOWN)]
            g.dma_start(out=v_gnall.rearrange("p (fc t) -> p fc t", t=TOWN),
                        in_=src).then_inc(s_gna.h, 16)
    s_gna.inc(16)
    GNA = s_gna.n

    for et in range(NCI):
        bi = et % 2
        sw = s_w4P[bi]
        @blk.sync
        def _(sy, et=et, bi=bi, fe=s_fe.n, hv=H_DONE, sw=sw, fmp=s_fmm.n,
              avdf=s_avd.n):
            if et == 0:
                sy.wait_ge(s_h.h, hv)
                sy.wait_ge(s_fmm.h, fmp)
                sy.wait_ge(s_avd.h, avdf)
            if et >= 2:
                sy.wait_ge(s_fe.h, fe)
            src = d_wot.rearrange("(fc e) p q -> fc e p q", e=NCI)[:, et].rearrange(
                "fc p q -> p fc q")
            sy.dma_start(out=v_wot[bi].rearrange("p (fc q) -> p fc q", q=128),
                         in_=src).then_inc(sw.h, 16)
            src = d_agd_o.ap().rearrange("(oc e) p q -> oc e p q", e=NCI)[0:OT, et].rearrange(
                "oc p q -> p oc q")
            sy.dma_start(out=v_wdt[bi].rearrange("p (oc q) -> p oc q", q=128),
                         in_=src).then_inc(sw.h, 16)
        sw.inc(32)
        wv = sw.n
        for tch in tchs:
            bk_ = tch % 2
            @blk.tensor
            def _(t, et=et, bi=bi, tch=tch, bk_=bk_, wv=wv, gna=GNA, hv=H_DONE,
                  fe=s_fe.n, TW=TW, sw=sw):
                t.wait_ge(sw.h, wv)
                t.wait_ge(s_gna.h, gna)
                t.wait_ge(s_h.h, hv)
                t.wait_ge(s_fe.h, fe)
                for fc in range(NCI):
                    t.matmul(bank(bk_, TW), lhsT=v_wot[bi][:, fc * 128:(fc + 1) * 128],
                             rhs=v_gnall[:, fc * TOWN + tch * TW:fc * TOWN + (tch + 1) * TW],
                             start=(fc == 0), stop=False)
                for oc in range(OT):
                    ins = t.matmul(bank(bk_, TW), lhsT=v_wdt[bi][:, oc * 128:(oc + 1) * 128],
                                   rhs=h_sl(oc, tch * TW, TW),
                                   start=False, stop=(oc == OT - 1))
                ins.then_inc(s_fmm.h, 1)
            s_fmm.inc()
            @blk.scalar
            def _(c, fm=s_fmm.n, tch=tch, bk_=bk_, ftp=s_ftp.n, TW=TW):
                c.wait_ge(s_fmm.h, fm)
                c.wait_ge(s_ftp.h, ftp)
                c.copy(out=fmt[:, tch * TC:tch * TC + TW], in_=bank(bk_, TW)).then_inc(s_fe.h, 1)
            s_fe.inc()
            @blk.tensor
            def _(t, fe=s_fe.n, tch=tch, fte=s_fte.n, TW=TW):
                t.wait_ge(s_fe.h, fe)
                t.wait_ge(s_fte.h, fte)
                for tsub in range(TW // 128):
                    ins = t.transpose(bankb(2)[:, tsub * 128:(tsub + 1) * 128],
                                      in_=fmt[:, tch * TC + tsub * 128:tch * TC + (tsub + 1) * 128],
                                      identity=ident[:, :])
                ins.then_inc(s_ftp.h, 1)
            s_ftp.inc()
            @blk.scalar
            def _(c, tv=s_ftp.n, et=et, tch=tch, TW=TW):
                c.wait_ge(s_ftp.h, tv)
                nts = TW // 128
                src = bankb(2)[:, 0:nts * 128].rearrange("p (s q) -> p s q", q=128)
                dst = v_outr.rearrange("p (tt e) -> p tt e", e=D)[
                    :, tch * (TC // 128):tch * (TC // 128) + nts, et * 128:(et + 1) * 128]
                c.copy(out=dst, in_=src).then_inc(s_fte.h, 1)
            s_fte.inc()
    FTE = s_fte.n

    for tg in range(TOWN // 128):
        @blk.sync
        def _(sy, tg=tg, fte=FTE, oa=s_oadd.n, fmp=s_fmm.n):
            sy.wait_ge(s_fte.h, fte)
            sy.wait_ge(s_fmm.h, fmp)
            if tg >= 1:
                sy.wait_ge(s_oadd.h, oa)
            dst = big1[:, (tg % 2) * 8192:(tg % 2) * 8192 + 4096].bitcast(F32)
            sy.dma_start(out=dst, in_=d_xo[tg * 128:(tg + 1) * 128, :]).then_inc(s_xl.h, 16)
        s_xl.inc(16)
        @blk.vector
        def _(v, tg=tg, xl=s_xl.n, ov=s_out.n):
            v.wait_ge(s_xl.h, xl)
            if tg >= 2:
                v.wait_ge(s_out.h, ov)
            xt = big1[:, (tg % 2) * 8192:(tg % 2) * 8192 + 4096].bitcast(F32)
            ot_ = big1[:, 16384 + (tg % 2) * 4096:16384 + (tg % 2) * 4096 + 4096].bitcast(F32)
            v.tensor_tensor(out=ot_, in0=v_outr[:, tg * D:(tg + 1) * D],
                            in1=xt, op=ALU.add).then_inc(s_oadd.h, 1)
        s_oadd.inc()
        @blk.sync
        def _(sy, tg=tg, oa=s_oadd.n):
            sy.wait_ge(s_oadd.h, oa)
            src = big1[:, 16384 + (tg % 2) * 4096:16384 + (tg % 2) * 4096 + 4096].bitcast(F32)
            sy.dma_start(out=d_out[tg * 128:(tg + 1) * 128, :], in_=src).then_inc(s_out.h, 16)
        s_out.inc(16)

    if "gnall" in taps:
        @blk.sync
        def _(sy, gna=GNA):
            sy.wait_ge(s_gna.h, gna)
            sy.dma_start(out=tap_dram["gnall"], in_=v_gnall).then_inc(s_out.h, 16)
        s_out.inc(16)

    @blk.sync
    def _(sy, n=s_out.n):
        sy.wait_ge(s_out.h, n)

    _replay()
    es.close()
    return nc, tap_dram


def prep_inputs(geo, L0_attn_x, L0_attn_W_qkv, L0_attn_W_o, L0_attn_lambda_q1,
                L0_attn_lambda_k1, L0_attn_lambda_q2, L0_attn_lambda_k2,
                L0_attn_gn_gamma, L0_attn_gn_beta, L0_ffn_lut, L0_ffn_gate_idx,
                L0_ffn_up_idx, L0_ffn_down_idx):
    g_ = geo
    D, HD, HPC, TOWN, CSH, DSH = g_.D, g_.HD, g_.HPC, g_.TOWN, g_.CSH, g_.DSH
    x = np.ascontiguousarray(np.asarray(L0_attn_x, dtype=np.float32))
    wqkv = np.asarray(L0_attn_W_qkv, dtype=np.float32)
    wo = np.ascontiguousarray(np.asarray(L0_attn_W_o, dtype=np.float32))
    lam4 = np.stack([np.asarray(a, dtype=np.float32).reshape(HD) for a in
                     (L0_attn_lambda_q1, L0_attn_lambda_k1,
                      L0_attn_lambda_q2, L0_attn_lambda_k2)], axis=0)
    gamma = np.asarray(L0_attn_gn_gamma, dtype=np.float32)
    beta = np.asarray(L0_attn_gn_beta, dtype=np.float32)
    lut = np.asarray(L0_ffn_lut, dtype=np.float32).reshape(1, 16)
    gidx = np.asarray(L0_ffn_gate_idx, dtype=np.int32)
    uidx = np.asarray(L0_ffn_up_idx, dtype=np.int32)
    didx = np.asarray(L0_ffn_down_idx, dtype=np.int32)
    didx_pad = np.zeros((D, NCORE * DSH), dtype=np.int32)
    didx_pad[:, :g_.INTER] = didx

    in_maps = []
    for core in range(NCORE):
        b = core // 4
        u = core % 4
        wq = wqkv[0 * D + HPC * u * HD: 0 * D + HPC * (u + 1) * HD]
        wk = wqkv[1 * D + HPC * u * HD: 1 * D + HPC * (u + 1) * HD]
        wv = wqkv[2 * D + HPC * u * HD: 2 * D + HPC * (u + 1) * HD]
        im = {
            "x": np.ascontiguousarray(x[b]),
            "xo": np.ascontiguousarray(x[b, u * TOWN:(u + 1) * TOWN]),
            "wqkv": np.ascontiguousarray(np.concatenate([wq, wk, wv], axis=0)),
            "wo": wo,
            "lam4": np.ascontiguousarray(lam4),
            "gnw": np.ascontiguousarray(
                np.concatenate([gamma[HPC * u:HPC * (u + 1)],
                                beta[HPC * u:HPC * (u + 1)]], axis=0)),
            "lut": np.ascontiguousarray(lut),
            "gidx": np.ascontiguousarray(gidx[:, CSH * core:CSH * (core + 1)]),
            "uidx": np.ascontiguousarray(uidx[:, CSH * core:CSH * (core + 1)]),
            "didx": np.ascontiguousarray(didx_pad[:, DSH * core:DSH * (core + 1)]),
        }
        in_maps.append(im)
    return in_maps


def kernel(**inputs):
    geo = Geo(mini=False)
    in_maps = prep_inputs(geo, **inputs)
    nc, _ = build(geo)
    lower_extended_insts(nc)
    res = run_bass_kernel_spmd(nc, in_maps, core_ids=list(range(NCORE)))
    out = np.empty((geo.B, geo.S, geo.D), dtype=np.float32)
    for core in range(NCORE):
        b = core // 4
        u = core % 4
        out[b, u * geo.TOWN:(u + 1) * geo.TOWN] = \
            res.results[core]["out"].reshape(geo.TOWN, geo.D)
    return out
